# version 05 (final)
# speedup vs baseline: 1.5835x; 1.0671x over previous
import math
import contextlib
import numpy as np
import concourse.bass as bass
import concourse.mybir as mybir
from concourse.bass_utils import run_bass_kernel_spmd
from concourse.bass import IndirectOffsetOnAxis

F32 = mybir.dt.float32
BF16 = mybir.dt.bfloat16
I32 = mybir.dt.int32
U32 = mybir.dt.uint32
AF = mybir.ActivationFunctionType
ALU = mybir.AluOpType
AX = mybir.AxisListType

NCORES = 8
B, N, D = 32, 2048, 1024
TLOC = 4 * N
TGLB = B * N
H0, DH0 = 16, 64
DFF0 = 4096
NEXP, CAP = 8, 2560
SLOTMAX0 = 136
EPS = 1e-5

_ENG_NAME_OF = {}

_ENG = {"v": ("vector", False), "s": ("scalar", False), "t": ("tensor", False),
        "g": ("gpsimd", False), "sy": ("sync", True), "gd": ("gpsimd", True),
        "sd": ("scalar", True), "vd": ("vector", True)}


class Buf:
    __slots__ = ("t", "writers", "readers", "name")

    def __init__(self, t, name=""):
        self.t = t
        self.name = name
        self.writers = {}
        self.readers = {}

    def ap(self):
        a = getattr(self.t, "ap", None)
        return self.t.ap() if callable(a) else self.t

    def __getitem__(self, idx):
        return self.t[idx]


class Sem:
    def __init__(self, stack, nc, name, is_dma=False, eng=None):
        self.h = stack.enter_context(nc.semaphore(name))
        self.n = 0
        self.name = name
        self.is_dma = is_dma
        self.eng = eng
        self.barrier = 0


class Prog:

    def __init__(self):
        self.nc = bass.Bass("TRN2", target_bir_lowering=False)
        self.stack = contextlib.ExitStack()
        self.sems = {}
        self.marks = {}
        self.consts = {}
        self.uid = 0

    def _sem(self, key):
        if key not in self.sems:
            engname = key.rsplit("_", 1)[0]
            s = Sem(self.stack, self.nc, "sem_" + key,
                    is_dma=key.endswith("_d"),
                    eng=getattr(self.nc, engname))
            self.sems[key] = s
            _ENG_NAME_OF[s] = engname
        return self.sems[key]

    def sbuf(self, name, shape, dt=F32, stack=None):
        t = (stack or self.stack).enter_context(
            self.nc.sbuf_tensor(name, shape, dt))
        return Buf(t, name)

    def psum(self, name, shape, dt=F32, stack=None):
        t = (stack or self.stack).enter_context(
            self.nc.psum_tensor(name, shape, dt))
        return Buf(t, name)

    def dram(self, name, shape, dt=F32, kind="Internal"):
        if kind == "Internal":
            t = self.nc.dram_tensor(name, list(shape), dt)
        else:
            t = self.nc.dram_tensor(name, list(shape), dt, kind=kind)
        return Buf(t.ap(), name)

    def do(self, e, reads, writes, fn, quiet=False):
        engname, dma = _ENG[e]
        eng = getattr(self.nc, engname)
        sem = self._sem(engname + ("_d" if dma else "_c"))
        marks = self.marks.setdefault(engname, {})
        assert not quiet or e == "t", "quiet only safe for tensor engine"
        need = {}

        def _need(s, v):
            if s.is_dma:
                if s.barrier < s.n:
                    s.eng.wait_ge(s.h, s.n)
                    s.barrier = s.n
                    self.marks.setdefault(
                        _ENG_NAME_OF[s], {})[s] = s.n
                v = s.barrier
            if need.get(s, -1) < v:
                need[s] = v

        for b in reads:
            for s, v in b.writers.items():
                _need(s, v)
        for b in writes:
            for s, v in b.writers.items():
                _need(s, v)
            for s, v in b.readers.items():
                _need(s, v)
        for s, v in need.items():
            if s is sem and engname == "tensor":
                continue
            if marks.get(s, -1) >= v:
                continue
            eng.wait_ge(s.h, v)
            marks[s] = v
        instr = fn(eng)
        if quiet:
            ver = sem.n + 1
        else:
            inc = 16 if dma else 1
            instr.then_inc(sem.h, inc)
            sem.n += inc
            ver = sem.n
        for b in writes:
            b.writers[sem] = ver
            b.readers = {}
        for b in reads:
            if b.readers.get(sem, -1) < ver:
                b.readers[sem] = ver
        return instr

    def alias(self, buf):
        self.uid += 1
        return Buf(buf.t, f"{buf.name}@{self.uid}")

    def dma_barrier(self, e):
        engname, dma = _ENG[e]
        s = self._sem(engname + "_d")
        if s.n > s.barrier:
            s.eng.wait_ge(s.h, s.n)
            s.barrier = s.n
            self.marks.setdefault(engname, {})[s] = s.n

    def drain_all(self):
        for engname in ("sync", "vector", "scalar", "tensor", "gpsimd"):
            eng = getattr(self.nc, engname)
            for key, s in list(self.sems.items()):
                if s.n > 0:
                    eng.wait_ge(s.h, s.n)
                    self.marks.setdefault(engname, {})[s] = s.n

    def close(self):
        self.stack.close()


def _slopes16():
    start = 2 ** (-2 ** (-(math.log2(16) - 3)))
    return [start * start ** i for i in range(16)]


def host_constants():
    c = {}
    p = np.arange(128)[:, None].astype(np.float64)
    slopes = np.array(_slopes16(), dtype=np.float64)
    tab = np.zeros((128, 16 * 19), np.float64)
    for h in range(16):
        for dl in range(-3, 16):
            tab[:, h * 19 + dl + 3] = slopes[h] * (p[:, 0] - 128 * dl)
    c["alibi_bias"] = tab.astype(np.float32)
    c["iota512"] = np.tile(np.arange(512, dtype=np.float32), 4).reshape(1, 2048)
    j = np.arange(512)[None, :]
    pi = np.arange(128)[:, None]
    c["causal4"] = np.concatenate(
        [(pi <= j - 128 * dd).astype(np.float32) for dd in range(4)], 0)
    c["causadd4"] = np.concatenate(
        [np.where(pi <= j - 128 * dd, 0.0, -30000.0).astype(np.float32)
         for dd in range(4)], 1)
    k = np.arange(128)[:, None]
    i = np.arange(128)[None, :]
    c["tri_strict"] = (k < i).astype(np.float32)
    c["iota8_row"] = np.tile(np.arange(8, dtype=np.float32), (128, 1))
    c["wr_ones_col"] = None
    return c


def prep_const_inputs(consts):
    return {k: v for k, v in consts.items() if v is not None}


def declare_load_consts(P, const_arrays, names):
    for name in names:
        arr = const_arrays[name]
        d = P.dram("c_" + name, arr.shape, F32, kind="ExternalInput")
        t = P.sbuf("sc_" + name, list(arr.shape), F32)
        P.do("sy", [d], [t], lambda e, d=d, t=t: e.dma_start(out=t[:], in_=d.ap()))
        P.consts[name] = t
    return P.consts


def make_identities(P):
    spj = P.sbuf("spj_id", [128, 128], F32)
    P.do("g", [], [spj], lambda e: e.iota(
        spj[:], pattern=[[-1, 128]], base=0, channel_multiplier=1,
        allow_small_or_imprecise_dtypes=True))
    for dt, nm in ((F32, "identf"), (BF16, "identb")):
        ident = P.sbuf(nm, [128, 128], dt)
        P.do("v", [spj], [ident],
             lambda e, i=ident: e.tensor_single_scalar(i[:], spj[:], 0.0,
                                                       ALU.is_equal))
        P.consts[nm] = ident


def emit_ln(P, x_dram, xn_tm_dram, xn_fm_dram, tag, fm32_dram=None):
    nc = P.nc
    with contextlib.ExitStack() as S:
        xt = [P.sbuf(f"ln{tag}_x{k}", [128, 1024], F32, S) for k in range(2)]
        sq = [P.sbuf(f"ln{tag}_sq{k}", [128, 1024], F32, S) for k in range(2)]
        stt = [P.sbuf(f"ln{tag}_st{k}", [128, 8], F32, S) for k in range(2)]
        xnf = [P.sbuf(f"ln{tag}_xnf{k}", [128, 1024], F32, S) for k in range(2)]
        xnb = [P.sbuf(f"ln{tag}_xnb{k}", [128, 1024], BF16, S) for k in range(2)]
        xtp = [P.sbuf(f"ln{tag}_xtp{k}", [128, 1024], BF16, S) for k in range(2)]
        ptr = [P.psum(f"ln{tag}_ptr{k}", [128, 128], BF16, S) for k in range(2)]
        ptrf = [P.psum(f"ln{tag}_ptrf{k}", [128, 128], F32, S) for k in range(2)]
        xtpf = [P.sbuf(f"ln{tag}_xtpf{k}", [128, 1024], F32, S)
                for k in range(2)]
        identb = P.consts["identb"]
        identf = P.consts["identf"]
        SUM, SQS, MU, VAR, LNV, RST = range(6)
        for i in range(64):
            pb = i % 2
            xti, xnbi, xtpi = xt[pb], xnb[pb], xtp[pb]
            sqi, sti, xnfi, xtpfi = sq[pb], stt[pb], xnf[pb], xtpf[pb]
            col = lambda c, sti=sti: sti[:, c:c + 1]
            P.do("sy", [x_dram], [xti],
                 lambda e, xti=xti, i=i: e.dma_start(
                     out=xti[:], in_=x_dram.t[i * 128:(i + 1) * 128, :]))
            P.do("v", [xti], [sti],
                 lambda e, xti=xti, col=col: e.reduce_sum(col(SUM), xti[:],
                                                          AX.X))
            P.do("s", [xti], [sqi, sti],
                 lambda e, xti=xti, sqi=sqi, col=col: e.activation(
                     sqi[:], xti[:], AF.Square, accum_out=col(SQS)))
            P.do("s", [sti], [sti],
                 lambda e, col=col: e.activation(col(MU), col(SUM),
                                                 AF.Identity,
                                                 scale=1.0 / 1024))
            P.do("s", [sti], [sti],
                 lambda e, col=col: e.activation(col(VAR), col(MU), AF.Square))
            P.do("v", [sti], [sti],
                 lambda e, col=col: e.tensor_scalar(
                     col(VAR), col(SQS), 1.0 / 1024, col(VAR),
                     ALU.mult, ALU.subtract))
            P.do("v", [sti], [sti],
                 lambda e, col=col: e.tensor_scalar(col(VAR), col(VAR), EPS,
                                                    None, ALU.add))
            P.do("s", [sti], [sti],
                 lambda e, col=col: e.activation(col(LNV), col(VAR), AF.Ln))
            P.do("s", [sti], [sti],
                 lambda e, col=col: e.activation(col(RST), col(LNV), AF.Exp,
                                                 scale=-0.5))
            P.do("v", [xti, sti], [xnfi],
                 lambda e, xti=xti, xnfi=xnfi, col=col: e.tensor_scalar(
                     xnfi[:], xti[:], col(MU), col(RST),
                     ALU.subtract, ALU.mult))
            P.do("v", [xnfi], [xnbi],
                 lambda e, xnfi=xnfi, xnbi=xnbi: e.tensor_copy(xnbi[:],
                                                               xnfi[:]))
            P.do("gd", [xnbi], [P.alias(xn_tm_dram)],
                 lambda e, xnbi=xnbi, i=i: e.dma_start(
                     out=xn_tm_dram.t[i * 128:(i + 1) * 128, :], in_=xnbi[:]))
            if fm32_dram is not None:
                for k in range(8):
                    pf = ptrf[k % 2]
                    P.do("t", [xnfi, identf], [pf],
                         lambda e, k=k, xnfi=xnfi, pf=pf: e.transpose(
                             pf[:], xnfi[:, k * 128:(k + 1) * 128], identf[:]))
                    P.do("v", [pf], [xtpfi],
                         lambda e, k=k, pf=pf, xtpfi=xtpfi: e.tensor_copy(
                             xtpfi[:, k * 128:(k + 1) * 128], pf[:]))
                P.do("gd", [xtpfi], [P.alias(fm32_dram)],
                     lambda e, i=i, xtpfi=xtpfi: e.dma_start(
                         out=bass.AP(fm32_dram.t.tensor, i * 128,
                                     [[TLOC, 128], [128 * TLOC, 8], [1, 128]]),
                         in_=xtpfi[:]))
            for k in range(8):
                pr = ptr[k % 2]
                P.do("t", [xnbi, identb], [pr],
                     lambda e, k=k, xnbi=xnbi, pr=pr: e.transpose(
                         pr[:], xnbi[:, k * 128:(k + 1) * 128], identb[:]))
                P.do("v", [pr], [xtpi],
                     lambda e, k=k, pr=pr, xtpi=xtpi: e.tensor_copy(
                         xtpi[:, k * 128:(k + 1) * 128], pr[:]))
            P.do("gd", [xtpi], [P.alias(xn_fm_dram)],
                 lambda e, i=i, xtpi=xtpi: e.dma_start(
                     out=bass.AP(xn_fm_dram.t.tensor, i * 128,
                                 [[TLOC, 128], [128 * TLOC, 8], [1, 128]]),
                     in_=xtpi[:]))
        P.drain_all()


def emit_att0(P, xn_fm_dram, wqkv_dram, wproj_dram, s0_dram, causal4_dram,
              causadd4_dram, nseq=4, nheads=16):
    nc = P.nc
    slopes = _slopes16()
    with contextlib.ExitStack() as S:
        wqkv = P.sbuf("a0_wqkv", [128, 8 * 3072], BF16, S)
        wproj = P.sbuf("a0_wproj", [128, 8 * 1024], BF16, S)
        for k in range(8):
            P.do("gd", [wqkv_dram], [wqkv],
                 lambda e, k=k: e.dma_start(
                     out=wqkv[:, k * 3072:(k + 1) * 3072],
                     in_=wqkv_dram.t[k * 128:(k + 1) * 128, :]))
            P.do("gd", [wproj_dram], [wproj],
                 lambda e, k=k: e.dma_start(
                     out=wproj[:, k * 1024:(k + 1) * 1024],
                     in_=wproj_dram.t[k * 128:(k + 1) * 128, :]))
        xfm = P.sbuf("a0_xfm", [128, 8 * 2048], BF16, S)
        Q = P.sbuf("a0_Q", [128, 2048], BF16, S)
        K = P.sbuf("a0_K", [128, 2048], BF16, S)
        V = P.sbuf("a0_V", [128, 16 * 1024], BF16, S)
        Kaug = P.sbuf("a0_Kaug", [65, 2048], BF16, S)
        Qaug = P.sbuf("a0_Qaug", [65, 2048], BF16, S)
        Vh = P.sbuf("a0_Vh", [128, 16 * 72], BF16, S)
        PT = [P.sbuf(f"a0_PT{z}", [128, 512], BF16, S) for z in range(2)]
        ctxs = P.sbuf("a0_ctxs", [65, 512], F32, S)
        lrow = [P.sbuf(f"a0_lrow{z}", [1, 512], F32, S) for z in range(2)]
        ctxa = P.sbuf("a0_ctxa", [128, 8 * 2048], BF16, S)
        pqk = P.psum("a0_pqk", [128, 512], F32, S)
        pst = [P.psum(f"a0_pst{k}", [128, 1024], F32, S) for k in range(2)]
        pctx = [P.psum(f"a0_pctx{z}", [65, 512], F32, S) for z in range(2)]
        pbc = P.psum("a0_pbc", [64, 512], F32, S)
        cmask = P.sbuf("a0_cmask", [128, 512], F32, S)
        sout = [P.sbuf(f"a0_sout{z}", [128, 512], F32, S) for z in range(1)]
        alibi = P.consts["alibi_bias"]
        iota512 = P.consts["iota512"]
        causb = P.sbuf("a0_causb", [128, 4 * 512], BF16, S)
        for dd in range(4):
            P.do("gd", [causal4_dram], [causb],
                 lambda e, dd=dd: e.dma_start(
                     out=causb[:, dd * 512:(dd + 1) * 512],
                     in_=causal4_dram.t[dd * 128:(dd + 1) * 128, :]))

        for s in range(nseq):
            for k in range(8):
                P.do("sy", [xn_fm_dram], [xfm],
                     lambda e, k=k: e.dma_start(
                         out=xfm[:, k * 2048:(k + 1) * 2048],
                         in_=xn_fm_dram.t[k * 128:(k + 1) * 128,
                                          s * 2048:(s + 1) * 2048]))
            for i in range(16):
                for vc in range(2):
                    for k in range(8):
                        P.do("t", [wqkv, xfm], [pqk],
                             lambda e, k=k, i=i, vc=vc:
                             e.matmul(pqk[:],
                                      xfm[:, k * 2048 + i * 128:k * 2048 + (i + 1) * 128],
                                      wqkv[:, k * 3072 + 2048 + vc * 512:
                                           k * 3072 + 2048 + (vc + 1) * 512],
                                      start=(k == 0), stop=(k == 7)),
                             quiet=(k < 7))
                    P.do("v", [pqk], [V],
                         lambda e, i=i, vc=vc: e.tensor_copy(
                             V[:, i * 1024 + vc * 512:i * 1024 + (vc + 1) * 512],
                             pqk[:]))
            for h in range(nheads):
                hp, hr = h // 2, (h % 2) * 64
                hz = h % 2
                Ka, Qa, Vha, ctxz, lrz, pcz = (Kaug, Qaug, Vh,
                                               ctxs, lrow[hz], pctx[hz])
                if h % 2 == 0:
                    for which, dst in ((0, Q), (1, K)):
                        for tt in range(4):
                            for k in range(8):
                                P.do("t", [wqkv, xfm], [pqk],
                                     lambda e, k=k, tt=tt, w=which:
                                     e.matmul(pqk[:],
                                              wqkv[:, k * 3072 + w * 1024 + hp * 128:
                                                   k * 3072 + w * 1024 + (hp + 1) * 128],
                                              xfm[:, k * 2048 + tt * 512:
                                                  k * 2048 + (tt + 1) * 512],
                                              start=(k == 0), stop=(k == 7)),
                                     quiet=(k < 7))
                            P.do("v", [pqk], [dst],
                                 lambda e, tt=tt, d=dst: e.tensor_copy(
                                     d[:, tt * 512:(tt + 1) * 512], pqk[:]))
                P.do("v", [K], [Ka],
                     lambda e: e.tensor_copy(Ka[0:64, :], K[hr:hr + 64, :]))
                P.do("v", [], [Ka],
                     lambda e: e.memset(Ka[64:65, :], -8.0 * slopes[h]))
                P.do("v", [Q], [Qa],
                     lambda e: e.tensor_copy(Qa[0:64, :], Qa[64:65, :])
                     if False else e.tensor_copy(Qa[0:64, :], Q[hr:hr + 64, :]))
                P.do("v", [iota512], [Qa],
                     lambda e: e.tensor_copy(Qa[64:65, :], iota512[:]))
                for kt in range(16):
                    P.do("v", [V], [Vha],
                         lambda e, kt=kt: e.tensor_copy(
                             Vha[:, kt * 72:kt * 72 + 64],
                             V[:, kt * 1024 + h * 64:kt * 1024 + (h + 1) * 64]))
                P.do("v", [], [Vha],
                     lambda e: e.memset(
                         Vha[:].t[:, 64:16 * 72:72] if False else Vha[:, 64:65],
                         1.0))
                for kt in range(1, 16):
                    P.do("v", [], [Vha],
                         lambda e, kt=kt: e.memset(
                             Vha[:, kt * 72 + 64:kt * 72 + 65], 1.0))
                for qt in range(4):
                    nkt = 4 * qt + 4
                    ngrp = (nkt + 1) // 2
                    for grp in range(ngrp):
                        gb = grp % 2
                        kts = [kt for kt in (2 * grp, 2 * grp + 1) if kt < nkt]
                        for z, kt in enumerate(kts):
                            P.do("t", [Ka, Qa], [pst[gb]],
                                 lambda e, kt=kt, z=z, gb=gb: e.matmul(
                                     pst[gb][:, z * 512:(z + 1) * 512],
                                     Ka[:, kt * 128:(kt + 1) * 128],
                                     Qa[:, qt * 512:(qt + 1) * 512],
                                     start=True, stop=True))
                        for z, kt in enumerate(kts):
                            dlt = 4 * qt - kt
                            bc = h * 19 + dlt + 3
                            preadd = _slopes16()[h] * 511 + 25 > 87
                            pz = kt % 2
                            PTz = PT[pz]
                            if dlt >= 1:
                                P.do("s", [pst[gb], alibi], [PTz],
                                     lambda e, z=z, gb=gb, bc=bc, PTz=PTz:
                                     e.activation(
                                         PTz[:],
                                         pst[gb][:, z * 512:(z + 1) * 512],
                                         AF.Exp, bias=alibi[:, bc:bc + 1],
                                         scale=0.125))
                            elif preadd:
                                dd = -dlt
                                P.do("sy", [causadd4_dram], [cmask],
                                     lambda e, dd=dd: e.dma_start(
                                         out=cmask[:],
                                         in_=causadd4_dram.t[:, dd * 512:(dd + 1) * 512]))
                                P.do("v", [pst[gb], cmask], [pst[gb]],
                                     lambda e, z=z, gb=gb: e.tensor_tensor(
                                         pst[gb][:, z * 512:(z + 1) * 512],
                                         pst[gb][:, z * 512:(z + 1) * 512],
                                         cmask[:],
                                         ALU.add))
                                P.do("s", [pst[gb], alibi], [PTz],
                                     lambda e, z=z, gb=gb, bc=bc, PTz=PTz:
                                     e.activation(
                                         PTz[:],
                                         pst[gb][:, z * 512:(z + 1) * 512],
                                         AF.Exp, bias=alibi[:, bc:bc + 1],
                                         scale=0.125))
                            else:
                                dd = -dlt
                                P.do("s", [pst[gb], alibi], [PTz],
                                     lambda e, z=z, gb=gb, bc=bc, PTz=PTz:
                                     e.activation(
                                         PTz[:],
                                         pst[gb][:, z * 512:(z + 1) * 512],
                                         AF.Exp, bias=alibi[:, bc:bc + 1],
                                         scale=0.125))
                                P.do("v", [PTz, causb], [PTz],
                                     lambda e, dd=dd, PTz=PTz: e.tensor_mul(
                                         PTz[:], PTz[:],
                                         causb[:, dd * 512:(dd + 1) * 512]))
                        for z, kt in enumerate(kts):
                            PTz = PT[kt % 2]
                            P.do("t", [Vha, PTz], [pcz],
                                 lambda e, kt=kt, PTz=PTz: e.matmul(
                                     pcz[:], Vha[:, kt * 72:kt * 72 + 65],
                                     PTz[:],
                                     start=(kt == 0), stop=(kt == nkt - 1)))
                    P.do("v", [pcz], [ctxz],
                         lambda e: e.tensor_copy(ctxz[:], pcz[:]))
                    P.do("v", [ctxz], [lrz],
                         lambda e: e.reciprocal(lrz[:], ctxz[64:65, :]))
                    P.do("t", [lrz, P.consts["onesf"]], [pbc],
                         lambda e: e.matmul(pbc[:],
                                            P.consts["onesf"][0:1, 0:64],
                                            lrz[:], start=True, stop=True))
                    P.do("v", [ctxz, pbc], [ctxa],
                         lambda e: e.tensor_tensor(
                             ctxa[hr:hr + 64, hp * 2048 + qt * 512:
                                  hp * 2048 + (qt + 1) * 512],
                             ctxz[0:64, :], pbc[:], ALU.mult))
            for i in range(16):
                pb = pst[i % 2]
                for vc in range(2):
                    for k in range(8):
                        P.do("t", [ctxa, wproj], [pb],
                             lambda e, k=k, i=i, vc=vc, pb=pb: e.matmul(
                                 pb[:, vc * 512:(vc + 1) * 512],
                                 ctxa[:, k * 2048 + i * 128:k * 2048 + (i + 1) * 128],
                                 wproj[:, k * 1024 + vc * 512:
                                       k * 1024 + (vc + 1) * 512],
                                 start=(k == 0), stop=(k == 7)),
                             quiet=(k < 7))
                    P.do("v", [pb], [sout[0]],
                         lambda e, vc=vc, pb=pb: e.tensor_copy(
                             sout[0][:], pb[:, vc * 512:(vc + 1) * 512]))
                    P.do("gd", [sout[0]], [P.alias(s0_dram)],
                         lambda e, i=i, s=s, vc=vc: e.dma_start(
                             out=s0_dram.t[s * 2048 + i * 128:
                                           s * 2048 + (i + 1) * 128,
                                           vc * 512:(vc + 1) * 512],
                             in_=sout[0][:]))
        P.drain_all()


def make_ones(P):
    ones = P.sbuf("onesf", [1, 512], F32)
    P.do("v", [], [ones], lambda e: e.memset(ones[:], 1.0))
    P.consts["onesf"] = ones


_CONSTS = host_constants()


def _core_seqs(c):
    return [c + 8 * j for j in range(4)]


def build_A1():
    P = Prog()
    xown = P.dram("xown", [TLOC, D], F32, kind="ExternalInput")
    wqkv = P.dram("wqkv", [D, 3 * D], F32, kind="ExternalInput")
    wproj = P.dram("wproj", [D, D], F32, kind="ExternalInput")
    xn0_tm = P.dram("xn0_tm", [TLOC, D], BF16, kind="ExternalOutput")
    xn0_fm = P.dram("xn0_fm", [D, TLOC], BF16, kind="ExternalOutput")
    s0 = P.dram("s0", [TLOC, D], F32, kind="ExternalOutput")
    declare_load_consts(P, _CONSTS, ["alibi_bias", "iota512"])
    causal4_d = P.dram("c_causal4", _CONSTS["causal4"].shape, F32,
                       kind="ExternalInput")
    causadd4_d = P.dram("c_causadd4", _CONSTS["causadd4"].shape, F32,
                        kind="ExternalInput")
    make_identities(P)
    make_ones(P)
    emit_ln(P, xown, xn0_tm, xn0_fm, "0")
    emit_att0(P, xn0_fm, wqkv, wproj, s0, causal4_d, causadd4_d)
    P.drain_all()
    nc = P.nc
    P.close()
    return nc


def _shard_inputs_A1(inputs):
    x = np.asarray(inputs["L0_attn_x"], dtype=np.float32).reshape(B, N, D)
    maps = []
    for c in range(NCORES):
        m = {"xown": np.ascontiguousarray(
                x[_core_seqs(c)].reshape(TLOC, D)),
             "wqkv": np.ascontiguousarray(inputs["L0_attn_w_qkv"], np.float32),
             "wproj": np.ascontiguousarray(inputs["L0_attn_w_proj"], np.float32)}
        for k in ("alibi_bias", "iota512", "causal4", "causadd4"):
            m["c_" + k] = _CONSTS[k]
        maps.append(m)
    return maps


_NC_CACHE = {}


def _get_nc(name, builder):
    if name not in _NC_CACHE:
        _NC_CACHE[name] = builder()
    return _NC_CACHE[name]


def kernel_multi(**inputs):
    inputs = {k: np.asarray(v) for k, v in inputs.items()}
    r1 = run_bass_kernel_spmd(_get_nc("A1", build_A1),
                              _shard_inputs_A1(inputs),
                              core_ids=list(range(NCORES))).results
    r2 = run_bass_kernel_spmd(_get_nc("A2", build_A2),
                              _shard_inputs_A2(inputs, r1),
                              core_ids=list(range(NCORES))).results
    x1 = [np.ascontiguousarray(r2[c]["x1"]) for c in range(NCORES)]
    rb1 = run_bass_kernel_spmd(_get_nc("B1", build_B1),
                               _shard_inputs_B1(inputs, x1),
                               core_ids=list(range(NCORES))).results
    rb2 = run_bass_kernel_spmd(_get_nc("B2", build_B2),
                               _shard_inputs_B2(inputs, x1, rb1),
                               core_ids=list(range(NCORES))).results
    out = np.zeros((B, N, D), np.float32)
    for c in range(NCORES):
        out[_core_seqs(c)] = np.asarray(rb2[c]["out"]).reshape(4, N, D)
    return out


@contextlib.contextmanager
def fori_shared(nc, regs, end, max_val):
    loop_id = nc.next_id()
    loop_start = f"fsh_{loop_id}_loop"
    loop_end = f"fsh_{loop_id}_end"
    nc.regs_mov(regs, 0)
    nc.br(loop_start)
    with nc.body(loop_start):
        yield bass.RuntimeValue(regs, min_val=0, max_val=max_val)
        nc.regs_alu(regs, regs, 1, op=ALU.add)
        nc.br_lt(regs, end, on_true=loop_start, on_false=loop_end)
    nc.switch_bb(loop_end)


def bias_col_tile(P, S, src_dram, nchunk, tag):
    raw = P.sbuf(f"b_{tag}_raw", [nchunk, 128], F32, S)
    out = P.sbuf(f"b_{tag}", [128, nchunk], F32, S)
    pt = P.psum(f"b_{tag}_pt", [128, nchunk], F32, S)
    P.do("sy", [src_dram], [raw],
         lambda e: e.dma_start(out=raw[:], in_=src_dram.ap()))
    P.do("t", [raw, P.consts["identf"]], [pt],
         lambda e: e.transpose(pt[0:128, 0:nchunk], raw[:],
                               P.consts["identf"][0:nchunk, 0:nchunk]))
    P.do("v", [pt], [out], lambda e: e.tensor_copy(out[:], pt[:]))
    return out


def emit_rt(P, xt_dram, wr_aug_dram, assign_dram, wetab_dram):
    nc = P.nc
    with contextlib.ExitStack() as S:
        wr = P.sbuf("rt_wr", [128, 8 * 9], F32, S)
        onec = P.sbuf("rt_onec", [128, 1], F32, S)
        epsc = P.sbuf("rt_epsc", [128, 1], F32, S)
        xt = [P.sbuf(f"rt_xt{k}", [128, 8 * 512], F32, S) for k in range(2)]
        sq = [P.sbuf(f"rt_sq{k}", [128, 512], F32, S) for k in range(2)]
        us = [P.sbuf(f"rt_us{k}", [9, 512], F32, S) for k in range(2)]
        ss = [P.sbuf(f"rt_ss{k}", [1, 512], F32, S) for k in range(2)]
        stt = [P.sbuf(f"rt_stt{j}", [128, 16], F32, S) for j in range(4)]
        srow = P.sbuf("rt_srow", [1, 9], F32, S)
        srep = P.sbuf("rt_srep", [128, 8], F32, S)
        logit = [P.sbuf(f"rt_logit{j}", [128, 8], F32, S) for j in range(4)]
        top8 = [P.sbuf(f"rt_top8{j}", [128, 8], F32, S) for j in range(4)]
        idx8 = [P.sbuf(f"rt_idx8{j}", [128, 8], U32, S) for j in range(4)]
        oh = [P.sbuf(f"rt_oh{j}", [128, 8], F32, S) for j in range(4)]
        ohT = [P.sbuf(f"rt_ohT{k}", [8, 512], F32, S) for k in range(2)]
        wpk = [P.sbuf(f"rt_wpk{j}", [128, 4], F32, S) for j in range(4)]
        pu = [P.psum(f"rt_pu{k}", [9, 512], F32, S) for k in range(2)]
        ps = [P.psum(f"rt_ps{k}", [1, 512], F32, S) for k in range(2)]
        pt = [P.psum(f"rt_pt{k}", [128, 128], F32, S) for k in range(2)]
        pto = [P.psum(f"rt_pto{k}", [8, 128], F32, S) for k in range(2)]
        iota8 = P.consts["iota8_row"]
        identf = P.consts["identf"]
        for k in range(8):
            P.do("sy", [wr_aug_dram], [wr],
                 lambda e, k=k: e.dma_start(
                     out=wr[:, k * 9:(k + 1) * 9],
                     in_=wr_aug_dram.t[k * 128:(k + 1) * 128, :]))
        P.do("v", [], [onec], lambda e: e.memset(onec[:], 1.0))
        P.do("v", [], [epsc], lambda e: e.memset(epsc[:], EPS))
        for k in range(8):
            P.do("t", [wr, onec], [pu[0]],
                 lambda e, k=k: e.matmul(pu[0][0:9, 0:1],
                                         wr[:, k * 9:(k + 1) * 9],
                                         onec[:], start=(k == 0), stop=(k == 7)),
                 quiet=(k < 7))
        P.do("v", [pu[0]], [us[0]],
             lambda e: e.tensor_copy(us[0][0:9, 0:1], pu[0][0:9, 0:1]))
        P.do("t", [us[0], identf], [pt[0]],
             lambda e: e.transpose(pt[0][0:1, 0:9], us[0][0:9, 0:1],
                                   identf[0:9, 0:9]))
        P.do("v", [pt[0]], [srow],
             lambda e: e.tensor_copy(srow[:], pt[0][0:1, 0:9]))
        P.do("t", [srow, P.consts["onesf"]], [pt[0]],
             lambda e: e.matmul(pt[0][:, 0:8], P.consts["onesf"][0:1, 0:128],
                                srow[0:1, 0:8], start=True, stop=True))
        P.do("v", [pt[0]], [srep],
             lambda e: e.tensor_copy(srep[:], pt[0][:, 0:8]))

        for g in range(128):
            pb = g % 2
            if g % 16 == 15:
                P.dma_barrier("gd")
            xg, usg, ssg, pug, psg = xt[pb], us[pb], ss[pb], pu[pb], ps[pb]
            for k in range(8):
                P.do("sy", [xt_dram], [xg],
                     lambda e, k=k, g=g: e.dma_start(
                         out=xg[:, k * 512:(k + 1) * 512],
                         in_=xt_dram.t[k * 128:(k + 1) * 128,
                                       g * 512:(g + 1) * 512]))
            for k in range(8):
                P.do("t", [wr, xg], [pug],
                     lambda e, k=k, pug=pug: e.matmul(
                         pug[:], wr[:, k * 9:(k + 1) * 9],
                         xg[:, k * 512:(k + 1) * 512],
                         start=(k == 0), stop=(k == 7)),
                     quiet=(k < 7))
            P.do("v", [pug], [usg],
                 lambda e, pug=pug, usg=usg: e.tensor_copy(usg[0:9, :], pug[:]))
            for k in range(8):
                sqk = sq[k % 2]
                P.do("s", [xg], [sqk],
                     lambda e, k=k, sqk=sqk: e.activation(
                         sqk[:], xg[:, k * 512:(k + 1) * 512], AF.Square))
                P.do("t", [onec, sqk], [psg],
                     lambda e, k=k, sqk=sqk, psg=psg: e.matmul(
                         psg[:], onec[:], sqk[:],
                         start=(k == 0), stop=(k == 7)))
            P.do("v", [psg], [ssg],
                 lambda e, psg=psg, ssg=ssg: e.tensor_copy(ssg[:], psg[:]))
            for sub in range(4):
                sp = pt[sub % 2]
                spo = pto[sub % 2]
                stb, lgb, t8b, i8b, ohb, wpb = (stt[sub], logit[sub],
                                                top8[sub], idx8[sub],
                                                oh[sub], wpk[sub])
                ohTg = ohT[pb]
                st = stb[:, 0:16]
                lg = lgb[:, 0:8]
                t8 = t8b[:, 0:8]
                i8 = i8b[:, 0:8]
                ohs = ohb[:, 0:8]
                wp = wpb[:, 0:4]
                P.do("t", [usg, identf], [sp],
                     lambda e, sub=sub, usg=usg, sp=sp: e.transpose(
                         sp[:, 0:9], usg[0:9, sub * 128:(sub + 1) * 128],
                         identf[0:9, 0:9]))
                P.do("t", [ssg, identf], [sp],
                     lambda e, sub=sub, ssg=ssg, sp=sp: e.transpose(
                         sp[:, 9:10], ssg[0:1, sub * 128:(sub + 1) * 128],
                         identf[0:1, 0:1]))
                P.do("v", [sp], [stb],
                     lambda e, st=st, sp=sp: e.tensor_copy(
                         st[:, 0:10], sp[:, 0:10]))
                MU, VAR, LNV, RST = 10, 11, 12, 13
                col = lambda c: st[:, c:c + 1]
                P.do("s", [stb], [stb],
                     lambda e, st=st, col=col: e.activation(
                         col(MU), st[:, 8:9], AF.Identity, scale=1.0 / 1024))
                P.do("s", [stb], [stb],
                     lambda e, col=col: e.activation(col(VAR), col(MU),
                                                     AF.Square))
                P.do("v", [stb], [stb],
                     lambda e, st=st, col=col: e.tensor_scalar(
                         col(VAR), st[:, 9:10], 1.0 / 1024, col(VAR),
                         ALU.mult, ALU.subtract))
                P.do("s", [stb, epsc], [stb],
                     lambda e, col=col: e.activation(col(LNV), col(VAR),
                                                     AF.Ln, bias=epsc[:]))
                P.do("s", [stb], [stb],
                     lambda e, col=col: e.activation(col(RST), col(LNV),
                                                     AF.Exp, scale=-0.5))
                P.do("v", [srep, stb], [lgb],
                     lambda e, lg=lg, col=col: e.tensor_scalar(
                         lg[:], srep[:], col(MU), None, ALU.mult))
                P.do("v", [stb, lgb], [lgb],
                     lambda e, st=st, lg=lg: e.tensor_sub(
                         lg[:], st[:, 0:8], lg[:]))
                P.do("v", [lgb, stb], [lgb],
                     lambda e, lg=lg, col=col: e.tensor_scalar(
                         lg[:], lg[:], col(RST), None, ALU.mult))
                P.do("v", [lgb], [t8b],
                     lambda e, lg=lg, t8=t8: e.max(t8[:], lg[:]))
                P.do("v", [t8b, lgb], [i8b],
                     lambda e, lg=lg, t8=t8, i8=i8: e.max_index(
                         i8[:], t8[:], lg[:]))
                P.do("v", [i8b], [wpb],
                     lambda e, i8=i8, wp=wp: e.tensor_copy(wp[:, 0:2],
                                                           i8[:, 0:2]))
                P.do("v", [t8b], [wpb],
                     lambda e, t8=t8, wp=wp: e.tensor_sub(
                         wp[:, 3:4], t8[:, 1:2], t8[:, 0:1]))
                P.do("s", [wpb], [wpb],
                     lambda e, wp=wp: e.activation(wp[:, 3:4], wp[:, 3:4],
                                                   AF.Exp))
                P.do("v", [wpb], [stb],
                     lambda e, st=st, wp=wp: e.tensor_scalar(
                         st[:, 14:15], wp[:, 3:4], 1.0, None, ALU.add))
                P.do("v", [stb], [wpb],
                     lambda e, st=st, wp=wp: e.reciprocal(wp[:, 2:3],
                                                          st[:, 14:15]))
                P.do("v", [wpb], [wpb],
                     lambda e, wp=wp: e.tensor_mul(wp[:, 3:4], wp[:, 3:4],
                                                   wp[:, 2:3]))
                P.do("v", [iota8, wpb], [ohb],
                     lambda e, ohs=ohs, wp=wp: e.tensor_scalar(
                         ohs[:], iota8[:], wp[:, 0:1], None, ALU.is_equal))
                P.do("v", [iota8, wpb, ohb], [ohb],
                     lambda e, ohs=ohs, wp=wp: e.scalar_tensor_tensor(
                         ohs[:], iota8[:], wp[:, 1:2], ohs[:],
                         ALU.is_equal, ALU.add))
                P.do("t", [ohb, identf], [spo],
                     lambda e, ohs=ohs, spo=spo: e.transpose(
                         spo[:, :], ohs[:], identf[:]))
                P.do("v", [spo], [ohTg],
                     lambda e, sub=sub, spo=spo, ohTg=ohTg: e.tensor_copy(
                         ohTg[:, sub * 128:(sub + 1) * 128], spo[:, :]))
                P.do("gd", [wpb], [P.alias(wetab_dram)],
                     lambda e, sub=sub, g=g, wp=wp: e.dma_start(
                         out=wetab_dram.t[g * 512 + sub * 128:
                                          g * 512 + (sub + 1) * 128, :],
                         in_=wp[:]))
            P.do("gd", [ohT[pb]], [P.alias(assign_dram)],
                 lambda e, g=g, pb=pb: e.dma_start(
                     out=assign_dram.t[:, g * 512:(g + 1) * 512],
                     in_=ohT[pb][:]))
        P.drain_all()


def emit_rp(P, assign_dram, slot_dram, fb_dram):
    nc = P.nc
    with contextlib.ExitStack() as S:
        elig = P.sbuf("rp_elig", [128, 512], F32, S)
        nfb = P.sbuf("rp_nfb", [128, 512], F32, S)
        fb = P.sbuf("rp_fb", [128, 512], F32, S)
        zer = P.sbuf("rp_zer", [128, 512], F32, S)
        sc = P.sbuf("rp_sc", [128, 512], F32, S)
        keep = P.sbuf("rp_keep", [128, 512], F32, S)
        slot = P.sbuf("rp_slot", [128, 512], F32, S)
        off = P.sbuf("rp_off", [128, 1], F32, S)
        poff = P.psum("rp_poff", [128, 1], F32, S)
        tri = P.consts["tri_strict"]
        P.do("v", [], [zer], lambda e: e.memset(zer[:], 0.0))
        P.do("v", [], [fb], lambda e: e.memset(fb[:], 0.0))
        P.do("v", [], [nfb], lambda e: e.memset(nfb[:], 1.0))
        for ex in range(8):
            P.do("sy", [assign_dram], [elig],
                 lambda e, ex=ex: e.dma_start(
                     out=elig[:],
                     in_=assign_dram.t[ex:ex + 1, :].rearrange(
                         "a (p f) -> (a p) f", p=128)))
            P.do("v", [elig, nfb], [elig],
                 lambda e: e.tensor_mul(elig[:], elig[:], nfb[:]))
            P.do("v", [elig, zer], [sc],
                 lambda e: e.tensor_tensor_scan(sc[:], elig[:], zer[:], 0.0,
                                                ALU.add, ALU.add))
            P.do("t", [tri, sc], [poff],
                 lambda e: e.matmul(poff[:], tri[:], sc[:, 511:512],
                                    start=True, stop=True))
            P.do("v", [poff], [off], lambda e: e.tensor_copy(off[:], poff[:]))
            P.do("v", [sc, off], [sc],
                 lambda e: e.tensor_scalar(sc[:], sc[:], off[:], None, ALU.add))
            P.do("v", [sc], [keep],
                 lambda e: e.tensor_scalar(keep[:], sc[:], float(CAP), None,
                                           ALU.is_le))
            P.do("v", [keep, elig], [keep],
                 lambda e: e.tensor_mul(keep[:], keep[:], elig[:]))
            P.do("v", [keep, sc], [slot],
                 lambda e: e.tensor_mul(slot[:], keep[:], sc[:]))
            P.do("v", [slot], [slot],
                 lambda e: e.tensor_scalar(slot[:], slot[:], 1.0, None,
                                           ALU.subtract))
            P.do("v", [elig, keep], [sc],
                 lambda e: e.tensor_sub(sc[:], elig[:], keep[:]))
            P.do("v", [fb, sc], [fb], lambda e: e.tensor_add(fb[:], fb[:], sc[:]))
            P.do("v", [fb], [nfb],
                 lambda e: e.tensor_scalar(nfb[:], fb[:], -1.0, 1.0, ALU.mult,
                                           ALU.add))
            P.do("gd", [slot], [P.alias(slot_dram)],
                 lambda e, ex=ex: e.dma_start(
                     out=slot_dram.t[ex:ex + 1, :].rearrange(
                     "a (p f) -> (a p) f", p=128),
                     in_=slot[:]))
        P.do("sy", [fb], [fb_dram],
             lambda e: e.dma_start(
                 out=fb_dram.t.rearrange("a (p f) -> (a p) f", p=128), in_=fb[:]))
        P.drain_all()


def emit_cp(P, slot_dram, fb_dram, wetab_dram, ownrows_dram,
            gidx_dram, rec_dram, ntiles_dram):
    nc = P.nc
    tri = P.consts["tri_strict"]
    ones = P.consts["onesf"]
    onescol = P.consts["onescol"]
    with contextlib.ExitStack() as S:
        orw = P.sbuf("cp_orw", [128, 1], I32, S)
        sg = P.sbuf("cp_sg", [128, 64], F32, S)
        km = [P.sbuf(f"cp_km{e}", [128, 64], F32, S) for e in range(8)]
        psl = [P.sbuf(f"cp_psl{e}", [128, 64], F32, S) for e in range(8)]
        sc = P.sbuf("cp_sc", [128, 64], F32, S)
        zer = P.sbuf("cp_zer", [128, 64], F32, S)
        off = P.sbuf("cp_off", [128, 1], F32, S)
        cnt = P.sbuf("cp_cnt", [128, 1], F32, S)
        nrow = P.sbuf("cp_nrow", [1, 8], F32, S)
        nint = P.sbuf("cp_ni", [1, 16], I32, S)
        tilf = P.sbuf("cp_tilf", [1, 8], F32, S)
        btf = P.sbuf("cp_btf", [1, 8], F32, S)
        ebase = P.sbuf("cp_eb", [128, 1], F32, S)
        tid = P.sbuf("cp_tid", [128, 64], F32, S)
        tidi = P.sbuf("cp_tidi", [128, 64], I32, S)
        pint = P.sbuf("cp_pint", [128, 64], I32, S)
        sel = P.sbuf("cp_sel", [128, 64], F32, S)
        trashw = P.sbuf("cp_trashw", [128, 64], F32, S)
        acc = P.sbuf("cp_acc", [128, 64], F32, S)
        msk = P.sbuf("cp_msk", [128, 64], F32, S)
        weg = P.sbuf("cp_weg", [128, 256], F32, S)
        fbg = P.sbuf("cp_fbg", [128, 64], F32, S)
        P1 = P.sbuf("cp_P1", [128, 64], F32, S)
        K1 = P.sbuf("cp_K1", [128, 64], F32, S)
        P2 = P.sbuf("cp_P2", [128, 64], F32, S)
        K2 = P.sbuf("cp_K2", [128, 64], F32, S)
        rec = P.sbuf("cp_rec", [128, 512], F32, S)
        poff = P.psum("cp_poff", [128, 1], F32, S)
        pcnt = P.psum("cp_pcnt", [1, 1], F32, S)
        pbc = P.psum("cp_pbc", [128, 1], F32, S)

        P.do("sy", [ownrows_dram], [orw],
             lambda e: e.dma_start(out=orw[:], in_=ownrows_dram.ap()))
        P.do("v", [], [zer], lambda e: e.memset(zer[:], 0.0))
        P.do("g", [], [tid],
             lambda e: e.iota(tid[:], pattern=[[1, 64]], base=0,
                              channel_multiplier=64,
                              allow_small_or_imprecise_dtypes=True))
        P.do("v", [tid], [tidi], lambda e: e.tensor_copy(tidi[:], tid[:]))
        P.do("g", [], [trashw],
             lambda e: e.iota(trashw[:], pattern=[[0, 64]],
                              base=SLOTMAX0 * 128, channel_multiplier=1,
                              allow_small_or_imprecise_dtypes=True))
        P.drain_all()
        for ex in range(8):
            P.do("gd", [slot_dram, orw], [sg],
                 lambda e, ex=ex: e.indirect_dma_start(
                     out=sg[:], out_offset=None,
                     in_=slot_dram.t.rearrange("a (r f) -> (a r) f", f=64),
                     in_offset=IndirectOffsetOnAxis(orw[:], 0),
                     element_offset=ex * TGLB))
            P.do("v", [sg], [km[ex]],
                 lambda e, ex=ex: e.tensor_scalar(km[ex][:], sg[:], 0.0, None,
                                                  ALU.is_ge))
            P.do("v", [km[ex], zer], [sc],
                 lambda e, ex=ex: e.tensor_tensor_scan(sc[:], km[ex][:], zer[:],
                                                       0.0, ALU.add, ALU.add))
            P.do("t", [tri, sc], [poff],
                 lambda e: e.matmul(poff[:], tri[:], sc[:, 63:64],
                                    start=True, stop=True))
            P.do("v", [poff], [off], lambda e: e.tensor_copy(off[:], poff[:]))
            P.do("v", [sc, km[ex]], [psl[ex]],
                 lambda e, ex=ex: e.tensor_sub(psl[ex][:], sc[:], km[ex][:]))
            P.do("v", [psl[ex], off], [psl[ex]],
                 lambda e, ex=ex: e.tensor_scalar(psl[ex][:], psl[ex][:], off[:],
                                                  None, ALU.add))
            P.do("v", [km[ex]], [cnt],
                 lambda e, ex=ex: e.reduce_sum(cnt[:], km[ex][:], AX.X))
            P.do("t", [cnt, onescol], [pcnt],
                 lambda e: e.matmul(pcnt[:], cnt[:], onescol[:],
                                    start=True, stop=True))
            P.do("v", [pcnt], [nrow],
                 lambda e, ex=ex: e.tensor_copy(nrow[:, ex:ex + 1], pcnt[:]))
        P.do("v", [nrow], [nint],
             lambda e: e.tensor_copy(nint[0:1, 0:8], nrow[:]))
        P.do("v", [nint], [nint],
             lambda e: e.tensor_scalar(nint[0:1, 0:8], nint[0:1, 0:8], 127, None,
                                       ALU.add))
        P.do("v", [nint], [nint],
             lambda e: e.tensor_scalar(nint[0:1, 0:8], nint[0:1, 0:8], 7, None,
                                       ALU.arith_shift_right))
        P.do("v", [nint], [nint],
             lambda e: e.tensor_scalar_max(nint[0:1, 0:8], nint[0:1, 0:8], 1))
        P.do("v", [nint], [tilf],
             lambda e: e.tensor_copy(tilf[:], nint[0:1, 0:8]))
        P.do("v", [tilf, zer], [btf],
             lambda e: e.tensor_tensor_scan(btf[:], tilf[:], zer[0:1, 0:8], 0.0,
                                            ALU.add, ALU.add))
        P.do("v", [btf, tilf], [btf],
             lambda e: e.tensor_sub(btf[:], btf[:], tilf[:]))
        P.do("v", [btf], [nint],
             lambda e: e.tensor_copy(nint[0:1, 8:16], btf[:]))
        P.do("sy", [nint], [ntiles_dram],
             lambda e: e.dma_start(out=ntiles_dram.ap(), in_=nint[:]))
        for ex in range(8):
            P.do("t", [btf, ones], [pbc],
                 lambda e, ex=ex: e.matmul(pbc[:], ones[0:1, 0:128],
                                           btf[0:1, ex:ex + 1],
                                           start=True, stop=True))
            P.do("v", [pbc], [ebase],
                 lambda e: e.tensor_copy(ebase[:], pbc[:]))
            P.do("v", [ebase], [ebase],
                 lambda e: e.tensor_scalar_mul(ebase[:], ebase[:], 128.0))
            P.do("v", [psl[ex], ebase], [psl[ex]],
                 lambda e, ex=ex: e.tensor_scalar(psl[ex][:], psl[ex][:],
                                                  ebase[:], None, ALU.add))
            P.do("v", [psl[ex], km[ex]], [sel],
                 lambda e, ex=ex: e.tensor_mul(sel[:], psl[ex][:], km[ex][:]))
            P.do("v", [km[ex], trashw], [acc],
                 lambda e, ex=ex: e.scalar_tensor_tensor(
                     acc[:], km[ex][:], 1.0, trashw[:], ALU.subtract, ALU.mult))
            P.do("v", [acc, sel], [sel],
                 lambda e: e.tensor_sub(sel[:], sel[:], acc[:]))
            P.do("v", [sel], [pint], lambda e: e.tensor_copy(pint[:], sel[:]))
            for f in range(64):
                P.do("gd", [pint, tidi], [P.alias(gidx_dram)],
                     lambda e, f=f: e.indirect_dma_start(
                         out=gidx_dram.ap(),
                         out_offset=IndirectOffsetOnAxis(pint[:, f:f + 1], 0),
                         in_=tidi[:, f:f + 1], in_offset=None))
            P.dma_barrier("gd")
        P.do("gd", [wetab_dram, orw], [weg],
             lambda e: e.indirect_dma_start(
                 out=weg[:], out_offset=None,
                 in_=wetab_dram.t.rearrange("(r f) c -> r (f c)", f=64),
                 in_offset=IndirectOffsetOnAxis(orw[:], 0)))
        P.do("gd", [fb_dram, orw], [fbg],
             lambda e: e.indirect_dma_start(
                 out=fbg[:], out_offset=None,
                 in_=fb_dram.t.rearrange("a (r f) -> (a r) f", f=64),
                 in_offset=IndirectOffsetOnAxis(orw[:], 0)))
        for which, Pt, Kt, ecol in ((0, P1, K1, 0), (1, P2, K2, 1)):
            P.do("v", [], [Pt], lambda e, Pt=Pt: e.memset(Pt[:], 0.0))
            P.do("v", [], [Kt], lambda e, Kt=Kt: e.memset(Kt[:], 0.0))
            for ex in range(8):
                P.do("v", [weg], [msk],
                     lambda e, ex=ex, ecol=ecol: e.tensor_single_scalar(
                         msk[:], weg[:, ecol::4], float(ex), ALU.is_equal))
                P.do("v", [msk, psl[ex], Pt], [Pt],
                     lambda e, ex=ex, Pt=Pt: e.scalar_tensor_tensor(
                         acc[:], psl[ex][:], 1.0, msk[:], ALU.mult, ALU.mult))
                P.do("v", [acc, Pt], [Pt],
                     lambda e, Pt=Pt: e.tensor_add(Pt[:], Pt[:], acc[:]))
                P.do("v", [msk, km[ex]], [acc],
                     lambda e, ex=ex: e.tensor_mul(acc[:], msk[:], km[ex][:]))
                P.do("v", [acc, Kt], [Kt],
                     lambda e, Kt=Kt: e.tensor_add(Kt[:], Kt[:], acc[:]))
            P.do("v", [Kt, Pt], [Pt],
                 lambda e, Pt=Pt, Kt=Kt: e.tensor_mul(Pt[:], Pt[:], Kt[:]))
            P.do("v", [Kt, Pt], [acc],
                 lambda e, Pt=Pt, Kt=Kt: e.scalar_tensor_tensor(
                     acc[:], Kt[:], -float(SLOTMAX0 * 128), Pt[:],
                     ALU.mult, ALU.add))
            P.do("v", [acc], [Pt],
                 lambda e, Pt=Pt: e.tensor_scalar(
                     Pt[:], acc[:], float(SLOTMAX0 * 128), None, ALU.add))
        P.do("v", [weg, K1], [K1],
             lambda e: e.tensor_mul(K1[:], K1[:], weg[:, 2::4]))
        P.do("v", [weg, K2], [K2],
             lambda e: e.tensor_mul(K2[:], K2[:], weg[:, 3::4]))
        P.do("v", [], [rec], lambda e: e.memset(rec[:], 0.0))
        for q, srcb in ((0, P1), (1, P2), (2, K1), (3, K2), (4, fbg)):
            P.do("v", [srcb, rec], [rec],
                 lambda e, q=q, srcb=srcb: e.tensor_copy(rec[:, q::8], srcb[:]))
        P.do("sy", [rec], [rec_dram],
             lambda e: e.dma_start(
                 out=rec_dram.t.rearrange("(p f) c -> p (f c)", f=64),
                 in_=rec[:]))
        P.drain_all()


def emit_yfb(P, xn_fm_dram, fw1_dram, fb1_dram, fw2_dram, fb2_dram, yfb_dram):
    nc = P.nc
    with contextlib.ExitStack() as S:
        w1 = P.sbuf("yf_w1", [128, 8 * 4096], BF16, S)
        w2 = P.sbuf("yf_w2", [128, 32 * 1024], BF16, S)
        for k in range(8):
            P.do("gd", [fw1_dram], [w1],
                 lambda e, k=k: e.dma_start(
                     out=w1[:, k * 4096:(k + 1) * 4096],
                     in_=fw1_dram.t[k * 128:(k + 1) * 128, :]))
        for k in range(32):
            P.do("gd", [fw2_dram], [w2],
                 lambda e, k=k: e.dma_start(
                     out=w2[:, k * 1024:(k + 1) * 1024],
                     in_=fw2_dram.t[k * 128:(k + 1) * 128, :]))
        b1c = bias_col_tile(P, S, fb1_dram, 32, "fb1")
        b2r = P.sbuf("yf_b2r", [1, 1024], F32, S)
        P.do("sy", [fb2_dram], [b2r],
             lambda e: e.dma_start(out=b2r[:], in_=fb2_dram.ap()))
        xn = P.sbuf("yf_xn", [128, 8 * 512], BF16, S)
        hf = P.sbuf("yf_hf", [128, 32 * 512], BF16, S)
        yo = P.sbuf("yf_yo", [128, 1024], F32, S)
        ph = [P.psum(f"yf_ph{k}", [128, 512], F32, S) for k in range(2)]
        py = P.psum("yf_py", [128, 512], F32, S)
        ones = P.consts["onesf"]
        for g in range(16):
            for k in range(8):
                P.do("sy", [xn_fm_dram], [xn],
                     lambda e, k=k, g=g: e.dma_start(
                         out=xn[:, k * 512:(k + 1) * 512],
                         in_=xn_fm_dram.t[k * 128:(k + 1) * 128,
                                          g * 512:(g + 1) * 512]))
            for hc in range(32):
                pb = hc % 2
                for k in range(8):
                    P.do("t", [w1, xn], [ph[pb]],
                         lambda e, k=k, hc=hc, pb=pb: e.matmul(
                             ph[pb][:],
                             w1[:, k * 4096 + hc * 128:k * 4096 + (hc + 1) * 128],
                             xn[:, k * 512:(k + 1) * 512],
                             start=(k == 0), stop=(k == 7)),
                         quiet=(k < 7))
                P.do("s", [ph[pb], b1c], [hf],
                     lambda e, hc=hc, pb=pb: e.activation(
                         hf[:, hc * 512:(hc + 1) * 512], ph[pb][:], AF.Gelu,
                         bias=b1c[:, hc:hc + 1]))
            for tt in range(4):
                for vc in range(2):
                    for k in range(32):
                        P.do("t", [hf, w2], [py],
                             lambda e, k=k, tt=tt, vc=vc: e.matmul(
                                 py[:],
                                 hf[:, k * 512 + tt * 128:k * 512 + (tt + 1) * 128],
                                 w2[:, k * 1024 + vc * 512:k * 1024 + (vc + 1) * 512],
                                 start=(k == 0), stop=False),
                             quiet=True)
                    P.do("t", [ones, b2r], [py],
                         lambda e, vc=vc: e.matmul(
                             py[:], ones[0:1, 0:128],
                             b2r[0:1, vc * 512:(vc + 1) * 512],
                             start=False, stop=True))
                    P.do("v", [py], [yo],
                         lambda e, vc=vc: e.tensor_copy(
                             yo[:, vc * 512:(vc + 1) * 512], py[:]))
                P.do("gd", [yo], [P.alias(yfb_dram)],
                     lambda e, g=g, tt=tt: e.dma_start(
                         out=yfb_dram.t[g * 512 + tt * 128:
                                        g * 512 + (tt + 1) * 128, :],
                         in_=yo[:]))
        P.drain_all()


def emit_xp0(P, xn_tm_dram, w1_dram, b1_dram, w2_dram, b2_dram, gidx_dram,
             ntiles_dram, yslot_dram, dff=4096, slotmax=None, tag="xp"):
    nc = P.nc
    if slotmax is None:
        slotmax = SLOTMAX0
    HC = dff // 128
    HALVES = max(1, dff // 2048)
    HPH = HC // HALVES
    xp = tag
    with contextlib.ExitStack() as S:
        w1 = P.sbuf(f"{xp}_w1", [128, 8 * dff], BF16, S)
        w2 = P.sbuf(f"{xp}_w2", [128, HC * 1024], BF16, S)
        b1r = P.sbuf(f"{xp}_b1r", [1, dff], F32, S)
        b2r = P.sbuf(f"{xp}_b2r", [1, 1024], F32, S)
        ntv = P.sbuf(f"{xp}_ntv", [1, 2], I32, S)
        idx = P.sbuf(f"{xp}_idx", [128, 1], I32, S)
        xg = P.sbuf(f"{xp}_xg", [128, 1024], BF16, S)
        xgf = P.sbuf(f"{xp}_xgf", [128, 1024], BF16, S)
        hf = P.sbuf(f"{xp}_hf", [128, dff], BF16, S)
        yo = P.sbuf(f"{xp}_yo", [128, 1024], F32, S)
        ptx = P.psum(f"{xp}_ptx", [128, 1024], BF16, S)
        ph = P.psum(f"{xp}_ph", [128, HPH * 128], F32, S)
        py = P.psum(f"{xp}_py", [128, 512], F32, S)
        identb = P.consts["identb"]
        ones = P.consts["onesf"]
        P.do("v", [], [xg], lambda e: e.memset(xg[:], 0.0))
        bsy = P.stack.enter_context(nc.semaphore(f"{xp}_bsy"))
        bgd = P.stack.enter_context(nc.semaphore(f"{xp}_bgd"))
        bwd = P.stack.enter_context(nc.semaphore(f"{xp}_bwd"))
        bnv = P.stack.enter_context(nc.semaphore(f"{xp}_bnv"))
        bt = P.stack.enter_context(nc.semaphore(f"{xp}_bt"))
        bv = P.stack.enter_context(nc.semaphore(f"{xp}_bv"))
        bs = P.stack.enter_context(nc.semaphore(f"{xp}_bs"))
        NT, NV, NS = 3 + HALVES, 3, HALVES
        P.drain_all()
        tl_regs = nc.alloc_registers(f"{xp}_tl")
        btl_regs = nc.alloc_registers(f"{xp}_btl")
        oloop_regs = nc.alloc_registers(f"{xp}_oloop")
        iloop_regs = nc.alloc_registers(f"{xp}_iloop")
        ENGNAMES = ("sync", "vector", "scalar", "tensor", "gpsimd")
        nc.regs_mov(tl_regs, 0)
        nc.regs_mov(btl_regs, 0)
        with fori_shared(nc, oloop_regs, 8, 7) as exv:
            tl0 = bass.RuntimeValue(tl_regs, min_val=0, max_val=slotmax)
            btl0 = bass.RuntimeValue(btl_regs, min_val=0, max_val=slotmax - 1)
            done = btl0 + tl0
            nc.scalar.wait_ge(bt, done * NT)
            nc.scalar.wait_ge(bv, done * NV)
            nc.scalar.wait_ge(bs, done * NS)
            nc.scalar.wait_ge(bgd, done * 16)
            nc.scalar.dma_start(
                out=ntv[0:1, 0:1],
                in_=ntiles_dram.t[0:1, bass.DynSlice(exv, 1)]).then_inc(bnv, 16)
            nc.scalar.dma_start(
                out=ntv[0:1, 1:2],
                in_=ntiles_dram.t[0:1, bass.DynSlice(exv + 8, 1)]).then_inc(bnv, 16)
            for engname in ENGNAMES:
                eng = getattr(nc, engname)
                eng.wait_ge(bnv, exv * 32 + 32)
                eng.reg_load(tl_regs[eng.engine], ntv[0:1, 0:1])
                eng.reg_load(btl_regs[eng.engine], ntv[0:1, 1:2])
            tl = bass.RuntimeValue(tl_regs, min_val=0, max_val=SLOTMAX0)
            btl = bass.RuntimeValue(btl_regs, min_val=0, max_val=SLOTMAX0 - 1)
            nc.gpsimd.wait_ge(bt, btl * NT)
            nc.gpsimd.dma_start(
                out=w1[:].rearrange("p (k c) -> p k c", k=8),
                in_=bass.AP(w1_dram.t.tensor, exv * (1024 * dff),
                            [[dff, 128], [128 * dff, 8], [1, dff]])
            ).then_inc(bwd, 16)
            nc.gpsimd.dma_start(
                out=w2[:].rearrange("p (k c) -> p k c", k=HC),
                in_=bass.AP(w2_dram.t.tensor, exv * (dff * 1024),
                            [[1024, 128], [128 * 1024, HC], [1, 1024]])
            ).then_inc(bwd, 16)
            nc.gpsimd.dma_start(
                out=b1r[:],
                in_=b1_dram.t[bass.DynSlice(exv, 1), :]).then_inc(bwd, 16)
            nc.gpsimd.dma_start(
                out=b2r[:],
                in_=b2_dram.t[bass.DynSlice(exv, 1), :]).then_inc(bwd, 16)
            wready = (exv + 1) * 64
            with fori_shared(nc, iloop_regs, tl_regs, slotmax - 1) as i:
                sr = nc.s_assert_within(btl + i, 0, slotmax - 1,
                                        skip_runtime_assert=True)
                nc.sync.wait_ge(bgd, sr * 16)
                nc.sync.dma_start(
                    out=idx[:],
                    in_=gidx_dram.t[bass.DynSlice(sr * 128, 128), :]
                ).then_inc(bsy, 16)
                nc.gpsimd.wait_ge(bsy, sr * 32 + 16)
                nc.gpsimd.wait_ge(bt, sr * NT)
                nc.gpsimd.indirect_dma_start(
                    out=xg[:], out_offset=None,
                    in_=xn_tm_dram.ap(),
                    in_offset=IndirectOffsetOnAxis(idx[:], 0),
                    bounds_check=TLOC - 1, oob_is_err=False,
                ).then_inc(bgd, 16)
                nc.tensor.wait_ge(bgd, sr * 16 + 16)
                nc.tensor.wait_ge(bv, sr * NV)
                for k in range(8):
                    mm = nc.tensor.transpose(
                        ptx[:, k * 128:(k + 1) * 128],
                        xg[:, k * 128:(k + 1) * 128], identb[:])
                mm.then_inc(bt, 1)
                nc.vector.wait_ge(bt, sr * NT + 1)
                for k in range(8):
                    cp = nc.vector.tensor_copy(
                        xgf[:, k * 128:(k + 1) * 128],
                        ptx[:, k * 128:(k + 1) * 128])
                cp.then_inc(bv, 1)
                nc.tensor.wait_ge(bwd, wready)
                for half in range(HALVES):
                    nc.tensor.wait_ge(bv, sr * NV + 1)
                    nc.tensor.wait_ge(bs, sr * NS + half)
                    for hc2 in range(HPH):
                        hc = half * HPH + hc2
                        for k in range(8):
                            nc.tensor.matmul(
                                ph[:, hc2 * 128:(hc2 + 1) * 128],
                                w1[:, k * dff + hc * 128:
                                   k * dff + (hc + 1) * 128],
                                xgf[:, k * 128:(k + 1) * 128],
                                start=(k == 0), stop=False)
                        mm = nc.tensor.matmul(
                            ph[:, hc2 * 128:(hc2 + 1) * 128],
                            b1r[0:1, hc * 128:(hc + 1) * 128],
                            ones[0:1, 0:128], start=False, stop=True)
                    mm.then_inc(bt, 1)
                    nc.scalar.wait_ge(bt, sr * NT + 2 + half)
                    ac = nc.scalar.activation(
                        hf[:, half * HPH * 128:(half + 1) * HPH * 128], ph[:],
                        AF.Gelu)
                    ac.then_inc(bs, 1)
                for vc in range(2):
                    nc.tensor.wait_ge(bs, sr * NS + HALVES)
                    if vc == 1:
                        nc.tensor.wait_ge(bv, sr * NV + 2)
                    for k in range(HC):
                        nc.tensor.matmul(
                            py[:],
                            hf[:, k * 128:(k + 1) * 128],
                            w2[:, k * 1024 + vc * 512:k * 1024 + (vc + 1) * 512],
                            start=(k == 0), stop=False)
                    nc.tensor.matmul(
                        py[:], ones[0:1, 0:128],
                        b2r[0:1, vc * 512:(vc + 1) * 512],
                        start=False, stop=True).then_inc(bt, 1)
                    nc.vector.wait_ge(bt, sr * NT + HALVES + 2 + vc)
                    nc.vector.wait_ge(bsy, sr * 32 + 16)
                    nc.vector.tensor_copy(
                        yo[:, vc * 512:(vc + 1) * 512],
                        py[:]).then_inc(bv, 1)
                nc.sync.wait_ge(bv, sr * NV + 3)
                nc.sync.dma_start(
                    out=yslot_dram.t[bass.DynSlice(sr * 128, 128), :],
                    in_=yo[:]).then_inc(bsy, 16)
        total_expr = tl + btl
        nc.sync.wait_ge(bsy, total_expr * 32)
        nc.sync.wait_ge(bnv, 8 * 32)
        nc.gpsimd.wait_ge(bgd, total_expr * 16)
        nc.gpsimd.wait_ge(bwd, 8 * 64)
        nc.tensor.wait_ge(bt, total_expr * NT)
        nc.vector.wait_ge(bv, total_expr * NV)
        nc.scalar.wait_ge(bs, total_expr * NS)
        nc.all_engine_barrier()
        for rs in (tl_regs, btl_regs, oloop_regs, iloop_regs):
            for h in rs:
                nc.free_register(h)
    P.drain_all()


def emit_gate0(P, xown_dram, s0_dram, yfb_dram, yslot_dram, rec_dram, x1_dram):
    nc = P.nc
    with contextlib.ExitStack() as S:
        rec = [P.sbuf(f"g0_rec{z}", [128, 8], F32, S) for z in range(2)]
        i1 = [P.sbuf(f"g0_i1{z}", [128, 1], I32, S) for z in range(2)]
        i2 = [P.sbuf(f"g0_i2{z}", [128, 1], I32, S) for z in range(2)]
        g1 = [P.sbuf(f"g0_g1{z}", [128, 1024], F32, S) for z in range(2)]
        g2 = [P.sbuf(f"g0_g2{z}", [128, 1024], F32, S) for z in range(2)]
        ym = [P.sbuf(f"g0_ym{z}", [128, 1024], F32, S) for z in range(2)]
        yf = [P.sbuf(f"g0_yf{z}", [128, 1024], F32, S) for z in range(2)]
        gg = [P.sbuf(f"g0_gg{z}", [128, 1024], F32, S) for z in range(2)]
        xo = [P.sbuf(f"g0_xo{z}", [128, 1024], F32, S) for z in range(2)]
        so = [P.sbuf(f"g0_so{z}", [128, 1024], F32, S) for z in range(2)]
        nfbc = [P.sbuf(f"g0_nfb{z}", [128, 1], F32, S) for z in range(2)]
        for z in range(2):
            P.do("v", [], [g1[z]], lambda e, z=z: e.memset(g1[z][:], 0.0))
            P.do("v", [], [g2[z]], lambda e, z=z: e.memset(g2[z][:], 0.0))
        for i in range(64):
            z = i % 2
            rz, i1z, i2z, g1z, g2z = rec[z], i1[z], i2[z], g1[z], g2[z]
            ymz, yfz, ggz, xoz, soz, nfz = (ym[z], yf[z], gg[z], xo[z],
                                            so[z], nfbc[z])
            P.do("sy", [rec_dram], [rz],
                 lambda e, i=i, rz=rz: e.dma_start(
                     out=rz[:], in_=rec_dram.t[i * 128:(i + 1) * 128, :]))
            P.do("v", [rz], [i1z],
                 lambda e, rz=rz, i1z=i1z: e.tensor_copy(i1z[:], rz[:, 0:1]))
            P.do("v", [rz], [i2z],
                 lambda e, rz=rz, i2z=i2z: e.tensor_copy(i2z[:], rz[:, 1:2]))
            P.do("gd", [i1z, yslot_dram], [g1z],
                 lambda e, i1z=i1z, g1z=g1z: e.indirect_dma_start(
                     out=g1z[:], out_offset=None, in_=yslot_dram.ap(),
                     in_offset=IndirectOffsetOnAxis(i1z[:], 0)))
            P.do("gd", [i2z, yslot_dram], [g2z],
                 lambda e, i2z=i2z, g2z=g2z: e.indirect_dma_start(
                     out=g2z[:], out_offset=None, in_=yslot_dram.ap(),
                     in_offset=IndirectOffsetOnAxis(i2z[:], 0)))
            P.do("sy", [yfb_dram], [yfz],
                 lambda e, i=i, yfz=yfz: e.dma_start(
                     out=yfz[:], in_=yfb_dram.t[i * 128:(i + 1) * 128, :]))
            P.do("sy", [xown_dram], [xoz],
                 lambda e, i=i, xoz=xoz: e.dma_start(
                     out=xoz[:], in_=xown_dram.t[i * 128:(i + 1) * 128, :]))
            P.do("sy", [s0_dram], [soz],
                 lambda e, i=i, soz=soz: e.dma_start(
                     out=soz[:], in_=s0_dram.t[i * 128:(i + 1) * 128, :]))
            P.do("v", [g1z, rz], [ymz],
                 lambda e, g1z=g1z, rz=rz, ymz=ymz: e.tensor_scalar(
                     ymz[:], g1z[:], rz[:, 2:3], None, ALU.mult))
            P.do("v", [g2z, rz, ymz], [ymz],
                 lambda e, g2z=g2z, rz=rz, ymz=ymz: e.scalar_tensor_tensor(
                     ymz[:], g2z[:], rz[:, 3:4], ymz[:], ALU.mult, ALU.add))
            P.do("v", [rz], [nfz],
                 lambda e, rz=rz, nfz=nfz: e.tensor_scalar(
                     nfz[:], rz[:, 4:5], -1.0, 1.0, ALU.mult, ALU.add))
            P.do("v", [ymz, nfz], [ymz],
                 lambda e, ymz=ymz, nfz=nfz: e.tensor_scalar(
                     ymz[:], ymz[:], nfz[:], None, ALU.mult))
            P.do("v", [yfz, rz, ymz], [ymz],
                 lambda e, yfz=yfz, rz=rz, ymz=ymz: e.scalar_tensor_tensor(
                     ymz[:], yfz[:], rz[:, 4:5], ymz[:], ALU.mult, ALU.add))
            P.do("s", [ymz], [ggz],
                 lambda e, ymz=ymz, ggz=ggz: e.activation(ggz[:], ymz[:],
                                                          AF.Silu))
            P.do("v", [soz, ggz], [ggz],
                 lambda e, soz=soz, ggz=ggz: e.tensor_mul(ggz[:], soz[:],
                                                          ggz[:]))
            P.do("v", [xoz, ggz], [xoz],
                 lambda e, xoz=xoz, ggz=ggz: e.tensor_add(xoz[:], xoz[:],
                                                          ggz[:]))
            P.do("gd", [xoz], [P.alias(x1_dram)],
                 lambda e, i=i, xoz=xoz: e.dma_start(
                     out=x1_dram.t[i * 128:(i + 1) * 128, :], in_=xoz[:]))
        P.drain_all()


def make_onescol(P):
    oc = P.sbuf("onescol", [128, 1], F32)
    P.do("v", [], [oc], lambda e: e.memset(oc[:], 1.0))
    P.consts["onescol"] = oc
    ob = P.sbuf("onescolb", [128, 1], BF16)
    P.do("v", [], [ob], lambda e: e.memset(ob[:], 1.0))
    P.consts["onescolb"] = ob


def build_A2():
    P = Prog()
    xt = P.dram("xt", [D, TGLB], F32, kind="ExternalInput")
    xown = P.dram("xown", [TLOC, D], F32, kind="ExternalInput")
    s0 = P.dram("s0", [TLOC, D], F32, kind="ExternalInput")
    xn0_tm = P.dram("xn0_tm", [TLOC, D], BF16, kind="ExternalInput")
    xn0_fm = P.dram("xn0_fm", [D, TLOC], BF16, kind="ExternalInput")
    wr_aug = P.dram("wr_aug", [D, 9], F32, kind="ExternalInput")
    w1e = P.dram("w1e", [8 * D, DFF0], F32, kind="ExternalInput")
    b1e = P.dram("b1e", [8, DFF0], F32, kind="ExternalInput")
    w2e = P.dram("w2e", [8 * DFF0, D], F32, kind="ExternalInput")
    b2e = P.dram("b2e", [8, D], F32, kind="ExternalInput")
    fw1 = P.dram("fw1", [D, DFF0], F32, kind="ExternalInput")
    fb1 = P.dram("fb1", [32, 128], F32, kind="ExternalInput")
    fw2 = P.dram("fw2", [DFF0, D], F32, kind="ExternalInput")
    fb2 = P.dram("fb2", [1, D], F32, kind="ExternalInput")
    ownrows = P.dram("ownrows", [128, 1], I32, kind="ExternalInput")
    x1 = P.dram("x1", [TLOC, D], F32, kind="ExternalOutput")

    assign = P.dram("assign", [8, TGLB], F32)
    wetab = P.dram("wetab", [TGLB, 4], F32)
    slot = P.dram("slot", [8, TGLB], F32)
    fbd = P.dram("fbd", [1, TGLB], F32)
    gidx = P.dram("gidx", [SLOTMAX0 * 128 + 128, 1], I32)
    rec = P.dram("recd", [TLOC, 8], F32)
    ntiles = P.dram("ntiles", [1, 16], I32)
    yfb = P.dram("yfbd", [TLOC, D], F32)
    yslot = P.dram("yslot", [SLOTMAX0 * 128 + 128, D], F32)

    declare_load_consts(P, _CONSTS, ["tri_strict", "iota8_row"])
    make_identities(P)
    make_ones(P)
    make_onescol(P)
    with contextlib.ExitStack() as S:
        big = P.sbuf("gidx_big", [128, SLOTMAX0 + 1], I32, S)
        P.do("v", [], [big], lambda e: e.memset(big[:], 1 << 29))
        P.do("sy", [big], [gidx],
             lambda e: e.dma_start(
                 out=gidx.t.rearrange("(a b) c -> a (b c)", a=128), in_=big[:]))
        zrow = P.sbuf("ys_zrow", [128, D], F32, S)
        P.do("v", [], [zrow], lambda e: e.memset(zrow[:], 0.0))
        for blk in range(SLOTMAX0 + 1):
            P.do("sy", [zrow], [yslot],
                 lambda e, blk=blk: e.dma_start(
                     out=yslot.t[blk * 128:(blk + 1) * 128, :], in_=zrow[:]))
        P.drain_all()
    emit_rt(P, xt, wr_aug, assign, wetab)
    emit_rp(P, assign, slot, fbd)
    emit_cp(P, slot, fbd, wetab, ownrows, gidx, rec, ntiles)
    emit_yfb(P, xn0_fm, fw1, fb1, fw2, fb2, yfb)
    emit_xp0(P, xn0_tm, w1e, b1e, w2e, b2e, gidx, ntiles, yslot)
    emit_gate0(P, xown, s0, yfb, yslot, rec, x1)
    P.drain_all()
    nc = P.nc
    P.close()
    return nc


def _shard_inputs_A2(inputs, a1_res):
    x = np.asarray(inputs["L0_attn_x"], dtype=np.float32).reshape(B, N, D)
    xt_full = np.ascontiguousarray(x.reshape(TGLB, D).T)
    wr_aug = np.ascontiguousarray(np.concatenate(
        [np.asarray(inputs["L0_ffn_Wr"], np.float32),
         np.ones((D, 1), np.float32)], 1))
    maps = []
    for c in range(NCORES):
        m = {
            "xt": xt_full,
            "xown": np.ascontiguousarray(x[_core_seqs(c)].reshape(TLOC, D)),
            "s0": a1_res[c]["s0"],
            "xn0_tm": a1_res[c]["xn0_tm"],
            "xn0_fm": a1_res[c]["xn0_fm"],
            "wr_aug": wr_aug,
            "w1e": np.ascontiguousarray(
                np.asarray(inputs["L0_ffn_W1"], np.float32).reshape(8 * D, DFF0)),
            "b1e": np.ascontiguousarray(np.asarray(inputs["L0_ffn_b1"], np.float32)),
            "w2e": np.ascontiguousarray(
                np.asarray(inputs["L0_ffn_W2"], np.float32).reshape(8 * DFF0, D)),
            "b2e": np.ascontiguousarray(np.asarray(inputs["L0_ffn_b2"], np.float32)),
            "fw1": np.ascontiguousarray(np.asarray(inputs["L0_ffn_fw1"], np.float32)),
            "fb1": np.ascontiguousarray(
                np.asarray(inputs["L0_ffn_fb1"], np.float32).reshape(32, 128)),
            "fw2": np.ascontiguousarray(np.asarray(inputs["L0_ffn_fw2"], np.float32)),
            "fb2": np.ascontiguousarray(
                np.asarray(inputs["L0_ffn_fb2"], np.float32).reshape(1, D)),
            "ownrows": np.asarray(
                [[(c + 8 * (p // 32)) * 32 + (p % 32)] for p in range(128)],
                np.int32),
            "c_tri_strict": _CONSTS["tri_strict"],
            "c_iota8_row": _CONSTS["iota8_row"],
        }
        maps.append(m)
    return maps


SLOTMAX1 = 72


def emit_qpxr(P, xnfm32_dram, wtq_aug_dram, qpd_dram, xred_dram):
    nc = P.nc
    identf = P.consts["identf"]
    with contextlib.ExitStack() as S:
        wa = P.sbuf("qx_wa", [128, 8 * 18], F32, S)
        xf = P.sbuf("qx_xf", [128, 8 * 512], F32, S)
        us = P.sbuf("qx_us", [18, 512], F32, S)
        xrt = P.sbuf("qx_xrt", [128, 18], F32, S)
        pu = P.psum("qx_pu", [18, 512], F32, S)
        pt = P.psum("qx_pt", [128, 18], F32, S)
        for k in range(8):
            P.do("sy", [wtq_aug_dram], [wa],
                 lambda e, k=k: e.dma_start(
                     out=wa[:, k * 18:(k + 1) * 18],
                     in_=wtq_aug_dram.t[k * 128:(k + 1) * 128, :]))
        for g in range(16):
            for k in range(8):
                P.do("sy", [xnfm32_dram], [xf],
                     lambda e, k=k, g=g: e.dma_start(
                         out=xf[:, k * 512:(k + 1) * 512],
                         in_=xnfm32_dram.t[k * 128:(k + 1) * 128,
                                           g * 512:(g + 1) * 512]))
            for k in range(8):
                P.do("t", [wa, xf], [pu],
                     lambda e, k=k: e.matmul(pu[:], wa[:, k * 18:(k + 1) * 18],
                                             xf[:, k * 512:(k + 1) * 512],
                                             start=(k == 0), stop=(k == 7)),
                     quiet=(k < 7))
            P.do("v", [pu], [us], lambda e: e.tensor_copy(us[:], pu[:]))
            P.do("gd", [us], [P.alias(qpd_dram)],
                 lambda e, g=g: e.dma_start(
                     out=qpd_dram.t[:, g * 512:(g + 1) * 512], in_=us[0:2, :]))
            for sub in range(4):
                P.do("t", [us, identf], [pt],
                     lambda e, sub=sub: e.transpose(
                         pt[:], us[:, sub * 128:(sub + 1) * 128],
                         identf[0:18, 0:18]))
                P.do("v", [pt], [xrt], lambda e: e.tensor_copy(xrt[:], pt[:]))
                P.do("gd", [xrt], [P.alias(xred_dram)],
                     lambda e, g=g, sub=sub: e.dma_start(
                         out=xred_dram.t[g * 512 + sub * 128:
                                         g * 512 + (sub + 1) * 128, :],
                         in_=xrt[:, 2:18]))
        P.drain_all()


def emit_att1(P, xn_tm_dram, xn_fm_dram, qpd_dram, wqkvo_dram, bqkvo_dram,
              wfq_aug_dram, causal4_dram, s1_dram, ctx_dram=None, q_dram=None,
              v_dram=None):
    nc = P.nc
    N2 = float(N) * float(N)
    with contextlib.ExitStack() as S:
        wbuf = P.sbuf("a1_w", [128, 8 * 1024], BF16, S)
        brow = P.sbuf("a1_brow", [1, 4 * 1024], F32, S)
        wfqa = P.sbuf("a1_wfqa", [3, 1024], F32, S)
        wfqb = P.sbuf("a1_wfqb", [3, 1024], BF16, S)
        Qqt = P.sbuf("a1_Qqt", [128, 8 * 512], BF16, S)
        xfm = Qqt
        K = P.sbuf("a1_K", [128, 8 * 2048], BF16, S)
        Vdc = P.sbuf("a1_Vdc", [128, 2 * 2048], BF16, S)
        PT = P.sbuf("a1_PT", [128, 16 * 512], BF16, S)
        qpx = P.sbuf("a1_qpx", [1, 2048], F32, S)
        qpy = P.sbuf("a1_qpy", [1, 2048], F32, S)
        drow = P.sbuf("a1_drow", [1, 2048], F32, S)
        srow = P.sbuf("a1_srow", [1, 2048], F32, S)
        stat = P.sbuf("a1_stat", [1, 32], F32, S)
        A4 = P.sbuf("a1_A4", [4, 2048], BF16, S)
        B4 = P.sbuf("a1_B4", [4, 2048], BF16, S)
        A2t = P.sbuf("a1_A2t", [2, 2048], BF16, S)
        A2y = P.sbuf("a1_A2y", [2, 2048], BF16, S)
        lb = P.sbuf("a1_lb", [128, 512], F32, S)
        B2x = P.sbuf("a1_B2x", [2, 2048], BF16, S)
        B2y = P.sbuf("a1_B2y", [2, 2048], BF16, S)
        RE = P.sbuf("a1_RE", [3, 2048], BF16, S)
        thr = P.sbuf("a1_thr", [128, 1], F32, S)
        vm = [P.sbuf(f"a1_vm{z}", [128, 512], F32, S) for z in range(2)]
        vm2 = [P.sbuf(f"a1_vm2{z}", [128, 512], F32, S) for z in range(2)]
        ctxb = [P.sbuf(f"a1_ctxs{z}", [128, 512], BF16, S) for z in range(2)]
        lrow = P.sbuf("a1_lrow", [1, 512], F32, S)
        lrec = P.sbuf("a1_lrec", [1, 512], F32, S)
        ctxi = P.sbuf("a1_ctxi", [128, 2 * 1024], BF16, S)
        xno = P.sbuf("a1_xno", [128, 1024], BF16, S)
        sout = P.sbuf("a1_sout", [128, 1024], F32, S)
        causb = P.sbuf("a1_causb", [128, 4 * 512], BF16, S)
        bqkc = P.sbuf("a1_bqkc", [128, 16], F32, S)
        braw = P.sbuf("a1_braw", [16, 128], F32, S)
        ps = [P.psum(f"a1_ps{k}", [128, 512], F32, S) for k in range(2)]
        pdd = [P.psum(f"a1_pdd{z}", [128, 512], F32, S) for z in range(2)]
        pxy = P.psum("a1_pxy", [128, 1024], F32, S)
        pl = P.psum("a1_pl", [1, 512], F32, S)
        pb = P.psum("a1_pb", [128, 512], F32, S)
        identf = P.consts["identf"]
        ones = P.consts["onesf"]
        P.do("sy", [bqkvo_dram], [brow],
             lambda e: e.dma_start(
                 out=brow[:],
                 in_=bqkvo_dram.t.rearrange("a b -> (a) (b)" if False else
                                            "a b -> () (a b)")))
        P.do("sy", [wfq_aug_dram], [wfqa],
             lambda e: e.dma_start(out=wfqa[:], in_=wfq_aug_dram.ap()))
        P.do("v", [wfqa], [wfqb], lambda e: e.tensor_copy(wfqb[:], wfqa[:]))
        P.do("gd", [causal4_dram], [causb],
             lambda e: e.dma_start(
                 out=causb[:],
                 in_=bass.AP(causal4_dram.t.tensor, 0,
                             [[512, 128], [128 * 512, 4], [1, 512]])))
        P.do("sy", [bqkvo_dram], [braw],
             lambda e: e.dma_start(
                 out=braw[:],
                 in_=bqkvo_dram.t[0:2, :].rearrange("a (k f) -> (a k) f", k=8)))
        P.do("t", [braw, P.consts["identf"]], [pdd[0]],
             lambda e: e.transpose(pdd[0][0:128, 0:16], braw[:],
                                   P.consts["identf"][0:16, 0:16]))
        P.do("v", [pdd[0]], [bqkc],
             lambda e: e.tensor_copy(bqkc[:], pdd[0][:, 0:16]))
        for s in range(4):
            for which, dst in ((0, None), (1, K)):
                for k in range(8):
                    P.do("gd", [wqkvo_dram], [wbuf],
                         lambda e, k=k, w=which: e.dma_start(
                             out=wbuf[:, k * 1024:(k + 1) * 1024],
                             in_=wqkvo_dram.t[k * 128:(k + 1) * 128,
                                              w * 1024:(w + 1) * 1024]))
                for tt in range(4):
                    P.do("sy", [xn_fm_dram], [xfm],
                         lambda e, tt=tt, s=s: e.dma_start(
                             out=xfm[:],
                             in_=bass.AP(xn_fm_dram.t.tensor,
                                         s * 2048 + tt * 512,
                                         [[TLOC, 128], [128 * TLOC, 8],
                                          [1, 512]])))
                    for oc in range(8):
                        pq = ps[oc % 2]
                        for k in range(8):
                            P.do("t", [wbuf, xfm], [pq],
                                 lambda e, k=k, oc=oc, pq=pq:
                                 e.matmul(pq[:],
                                          wbuf[:, k * 1024 + oc * 128:
                                               k * 1024 + (oc + 1) * 128],
                                          xfm[:, k * 512:(k + 1) * 512],
                                          start=(k == 0), stop=(k == 7)),
                                 quiet=(k < 7))
                        if dst is None:
                            cb = ctxb[oc % 2]
                            P.do("s", [pq, bqkc], [cb],
                                 lambda e, oc=oc, pq=pq, cb=cb: e.activation(
                                     cb[:], pq[:], AF.Identity,
                                     bias=bqkc[:, oc:oc + 1]))
                            P.do("gd", [cb], [P.alias(q_dram)],
                                 lambda e, oc=oc, tt=tt, cb=cb: e.dma_start(
                                     out=q_dram.t[oc * 128:(oc + 1) * 128,
                                                  tt * 512:(tt + 1) * 512],
                                     in_=cb[:]))
                        else:
                            P.do("s", [pq, bqkc], [dst],
                                 lambda e, oc=oc, tt=tt, d=dst, pq=pq:
                                 e.activation(
                                     d[:, oc * 2048 + tt * 512:
                                       oc * 2048 + (tt + 1) * 512],
                                     pq[:], AF.Identity,
                                     bias=bqkc[:, 8 + oc:9 + oc]))
            for k in range(8):
                P.do("gd", [wqkvo_dram], [wbuf],
                     lambda e, k=k: e.dma_start(
                         out=wbuf[:, k * 1024:(k + 1) * 1024],
                         in_=wqkvo_dram.t[k * 128:(k + 1) * 128, 2048:3072]))
            for i in range(16):
                xo = (i % 2) * 1024
                P.do("sy", [xn_fm_dram], [Qqt],
                     lambda e, i=i, s=s, xo=xo: e.dma_start(
                         out=Qqt[:, xo:xo + 1024],
                         in_=bass.AP(xn_fm_dram.t.tensor,
                                     s * 2048 + i * 128,
                                     [[TLOC, 128], [128 * TLOC, 8],
                                      [1, 128]])))
                for vc in range(2):
                    pq = ps[vc]
                    cb = ctxb[vc]
                    for k in range(8):
                        P.do("t", [wbuf, Qqt], [pq],
                             lambda e, k=k, vc=vc, pq=pq, xo=xo: e.matmul(
                                 pq[:],
                                 Qqt[:, xo + k * 128:xo + (k + 1) * 128],
                                 wbuf[:, k * 1024 + vc * 512:
                                      k * 1024 + (vc + 1) * 512],
                                 start=(k == 0), stop=False),
                             quiet=True)
                    P.do("t", [ones, brow], [pq],
                         lambda e, vc=vc, pq=pq: e.matmul(
                             pq[:], ones[0:1, 0:128],
                             brow[0:1, 2048 + vc * 512:2048 + (vc + 1) * 512],
                             start=False, stop=True))
                    P.do("v", [pq], [cb],
                         lambda e, pq=pq, cb=cb: e.tensor_copy(cb[:], pq[:]))
                    P.do("gd", [cb], [P.alias(v_dram)],
                         lambda e, i=i, vc=vc, cb=cb: e.dma_start(
                             out=v_dram.t[i * 128:(i + 1) * 128,
                                          vc * 512:(vc + 1) * 512],
                             in_=cb[:]))
            for k in range(8):
                P.do("gd", [wqkvo_dram], [wbuf],
                     lambda e, k=k: e.dma_start(
                         out=wbuf[:, k * 1024:(k + 1) * 1024],
                         in_=wqkvo_dram.t[k * 128:(k + 1) * 128, 3072:4096]))
            P.do("sy", [qpd_dram], [qpx],
                 lambda e, s=s: e.dma_start(
                     out=qpx[:], in_=qpd_dram.t[0:1, s * 2048:(s + 1) * 2048]))
            P.do("sy", [qpd_dram], [qpy],
                 lambda e, s=s: e.dma_start(
                     out=qpy[:], in_=qpd_dram.t[1:2, s * 2048:(s + 1) * 2048]))
            ST = stat
            cols = {"sx": 0, "sy": 1, "sxx": 2, "syy": 3, "sxy": 4, "sdd": 5,
                    "sdx": 6, "sdy": 7, "sd": 8, "s1": 9, "s2": 10, "mean": 11,
                    "var": 12, "thr": 13, "t1": 14, "t2": 15}
            C = lambda n: ST[0:1, cols[n]:cols[n] + 1]
            P.do("v", [qpx], [ST], lambda e: e.reduce_sum(C("sx"), qpx[:], AX.X))
            P.do("v", [qpy], [ST], lambda e: e.reduce_sum(C("sy"), qpy[:], AX.X))
            P.do("v", [qpx, qpy], [srow],
                 lambda e: e.tensor_mul(srow[:], qpx[:], qpy[:]))
            P.do("v", [srow], [ST], lambda e: e.reduce_sum(C("sxy"), srow[:], AX.X))
            P.do("v", [qpx], [drow], lambda e: e.tensor_mul(drow[:], qpx[:], qpx[:]))
            P.do("v", [drow], [ST], lambda e: e.reduce_sum(C("sxx"), drow[:], AX.X))
            P.do("v", [qpy], [srow],
                 lambda e: e.tensor_mul(srow[:], qpy[:], qpy[:]))
            P.do("v", [srow], [ST], lambda e: e.reduce_sum(C("syy"), srow[:], AX.X))
            P.do("v", [drow, srow], [drow],
                 lambda e: e.tensor_add(drow[:], drow[:], srow[:]))
            P.do("v", [drow, qpx], [srow],
                 lambda e: e.tensor_mul(srow[:], drow[:], qpx[:]))
            P.do("v", [srow], [ST], lambda e: e.reduce_sum(C("sdx"), srow[:], AX.X))
            P.do("v", [drow, qpy], [srow],
                 lambda e: e.tensor_mul(srow[:], drow[:], qpy[:]))
            P.do("v", [srow], [ST], lambda e: e.reduce_sum(C("sdy"), srow[:], AX.X))
            P.do("v", [drow, drow], [srow],
                 lambda e: e.tensor_mul(srow[:], drow[:], drow[:]))
            P.do("v", [srow], [ST], lambda e: e.reduce_sum(C("sdd"), srow[:], AX.X))
            P.do("v", [ST], [ST],
                 lambda e: e.tensor_add(C("sd"), C("sxx"), C("syy")))
            P.do("v", [ST], [ST], lambda e: e.tensor_mul(C("t1"), C("sx"), C("sx")))
            P.do("v", [ST], [ST], lambda e: e.tensor_mul(C("t2"), C("sy"), C("sy")))
            P.do("v", [ST], [ST], lambda e: e.tensor_add(C("t1"), C("t1"), C("t2")))
            P.do("v", [ST], [ST],
                 lambda e: e.tensor_scalar(C("s1"), C("sd"), 2.0 * N, C("t1"),
                                           ALU.mult, ALU.subtract))
            P.do("v", [ST], [ST],
                 lambda e: e.tensor_scalar(C("s1"), C("s1"), 1.0, C("t1"),
                                           ALU.mult, ALU.subtract))
            P.do("v", [ST], [ST], lambda e: e.tensor_mul(C("t1"), C("sd"), C("sd")))
            P.do("v", [ST], [ST],
                 lambda e: e.tensor_scalar(C("s2"), C("sdd"), 2.0 * N, None,
                                           ALU.mult))
            P.do("v", [ST], [ST],
                 lambda e: e.scalar_tensor_tensor(C("s2"), C("t1"), 2.0, C("s2"),
                                                  ALU.mult, ALU.add))
            P.do("v", [ST], [ST], lambda e: e.tensor_mul(C("t1"), C("sxx"), C("sxx")))
            P.do("v", [ST], [ST],
                 lambda e: e.scalar_tensor_tensor(C("s2"), C("t1"), 4.0, C("s2"),
                                                  ALU.mult, ALU.add))
            P.do("v", [ST], [ST], lambda e: e.tensor_mul(C("t1"), C("sxy"), C("sxy")))
            P.do("v", [ST], [ST],
                 lambda e: e.scalar_tensor_tensor(C("s2"), C("t1"), 8.0, C("s2"),
                                                  ALU.mult, ALU.add))
            P.do("v", [ST], [ST], lambda e: e.tensor_mul(C("t1"), C("syy"), C("syy")))
            P.do("v", [ST], [ST],
                 lambda e: e.scalar_tensor_tensor(C("s2"), C("t1"), 4.0, C("s2"),
                                                  ALU.mult, ALU.add))
            P.do("v", [ST], [ST], lambda e: e.tensor_mul(C("t1"), C("sdx"), C("sx")))
            P.do("v", [ST], [ST],
                 lambda e: e.scalar_tensor_tensor(C("s2"), C("t1"), -8.0, C("s2"),
                                                  ALU.mult, ALU.add))
            P.do("v", [ST], [ST], lambda e: e.tensor_mul(C("t1"), C("sdy"), C("sy")))
            P.do("v", [ST], [ST],
                 lambda e: e.scalar_tensor_tensor(C("s2"), C("t1"), -8.0, C("s2"),
                                                  ALU.mult, ALU.add))
            P.do("v", [ST], [ST],
                 lambda e: e.tensor_scalar(C("mean"), C("s1"), 1.0 / N2, None,
                                           ALU.mult))
            P.do("v", [ST], [ST], lambda e: e.tensor_mul(C("t1"), C("s1"), C("s1")))
            P.do("v", [ST], [ST],
                 lambda e: e.tensor_scalar(C("t1"), C("t1"), 1.0 / N2, None,
                                           ALU.mult))
            P.do("v", [ST], [ST], lambda e: e.tensor_sub(C("var"), C("s2"), C("t1")))
            P.do("v", [ST], [ST],
                 lambda e: e.tensor_scalar(C("var"), C("var"), 1.0 / (N2 - 1.0),
                                           None, ALU.mult))
            P.do("s", [ST], [ST], lambda e: e.activation(C("t2"), C("var"), AF.Sqrt))
            P.do("v", [ST], [ST],
                 lambda e: e.tensor_scalar(C("thr"), C("t2"), 1.25, C("mean"),
                                           ALU.mult, ALU.add))
            P.do("t", [ST, ones], [pb],
                 lambda e: e.matmul(pb[:, 0:1], ones[0:1, 0:128], C("thr"),
                                    start=True, stop=True))
            P.do("v", [pb], [thr], lambda e: e.tensor_copy(thr[:], pb[:, 0:1]))
            tmp = srow
            P.do("v", [qpx], [A4],
                 lambda e: e.tensor_scalar_mul(A4[0:1, :], qpx[:], -2.0))
            P.do("v", [qpy], [tmp],
                 lambda e: e.tensor_scalar_mul(tmp[:], qpy[:], -2.0))
            P.do("gd", [tmp], [A4],
                 lambda e: e.dma_start(out=A4[1:2, :], in_=tmp[:]))
            P.do("gd", [drow], [A4],
                 lambda e: e.dma_start(out=A4[2:3, :], in_=drow[:]))
            P.do("v", [], [tmp], lambda e: e.memset(tmp[:], 1.0))
            P.do("gd", [tmp], [A4],
                 lambda e: e.dma_start(out=A4[3:4, :], in_=tmp[:]))
            P.do("v", [qpx], [B4], lambda e: e.tensor_copy(B4[0:1, :], qpx[:]))
            P.do("gd", [qpy], [B4],
                 lambda e: e.dma_start(out=B4[1:2, :], in_=qpy[:]))
            P.do("gd", [tmp], [B4],
                 lambda e: e.dma_start(out=B4[2:3, :], in_=tmp[:]))
            P.do("gd", [drow], [B4],
                 lambda e: e.dma_start(out=B4[3:4, :], in_=drow[:]))
            P.do("v", [qpx], [A2t],
                 lambda e: e.tensor_scalar_mul(A2t[0:1, :], qpx[:], -1.0))
            P.do("gd", [tmp], [A2t],
                 lambda e: e.dma_start(out=A2t[1:2, :], in_=tmp[:]))
            P.do("v", [tmp], [B2x],
                 lambda e: e.tensor_copy(B2x[0:1, :], tmp[:]))
            P.do("gd", [qpx], [B2x],
                 lambda e: e.dma_start(out=B2x[1:2, :], in_=qpx[:]))
            P.do("v", [tmp], [B2y],
                 lambda e: e.tensor_copy(B2y[0:1, :], tmp[:]))
            P.do("gd", [qpy], [B2y],
                 lambda e: e.dma_start(out=B2y[1:2, :], in_=qpy[:]))
            P.do("v", [qpy], [A2y],
                 lambda e: e.tensor_scalar_mul(A2y[0:1, :], qpy[:], -1.0))
            P.do("gd", [tmp], [A2y],
                 lambda e: e.dma_start(out=A2y[1:2, :], in_=tmp[:]))
            P.do("s", [drow], [drow],
                 lambda e: e.activation(drow[:], drow[:], AF.Sqrt))
            P.do("v", [drow], [drow],
                 lambda e: e.tensor_scalar_max(drow[:], drow[:], 1e-12))
            P.do("v", [drow], [drow],
                 lambda e: e.reciprocal(drow[:], drow[:]))
            P.do("v", [drow], [drow],
                 lambda e: e.tensor_scalar_mul(drow[:], drow[:], 0.1))
            P.do("v", [qpx, drow], [RE],
                 lambda e: e.tensor_mul(RE[0:1, :], qpx[:], drow[:]))
            P.do("v", [qpy, drow], [tmp],
                 lambda e: e.tensor_mul(tmp[:], qpy[:], drow[:]))
            P.do("gd", [tmp], [RE],
                 lambda e: e.dma_start(out=RE[1:2, :], in_=tmp[:]))
            P.do("v", [], [tmp], lambda e: e.memset(tmp[:], 0.1))
            P.do("gd", [tmp], [RE],
                 lambda e: e.dma_start(out=RE[2:3, :], in_=tmp[:]))
            P.drain_all()
            for qt in range(4):
                nkt = 4 * qt + 4
                for k in range(8):
                    P.do("sy", [q_dram], [Qqt],
                         lambda e, k=k, qt=qt: e.dma_start(
                             out=Qqt[:, k * 512:(k + 1) * 512],
                             in_=q_dram.t[k * 128:(k + 1) * 128,
                                          qt * 512:(qt + 1) * 512]))
                for kt in range(nkt):
                    gb = kt % 2
                    for k in range(8):
                        P.do("t", [K, Qqt], [ps[gb]],
                             lambda e, k=k, kt=kt, gb=gb: e.matmul(
                                 ps[gb][:],
                                 K[:, k * 2048 + kt * 128:k * 2048 + (kt + 1) * 128],
                                 Qqt[:, k * 512:(k + 1) * 512],
                                 start=(k == 0), stop=(k == 7)),
                             quiet=(k < 7))
                    pz, vmz, vm2z = pdd[gb], vm[gb], vm2[gb]
                    P.do("t", [A4, B4], [pz],
                         lambda e, kt=kt, qt=qt, pz=pz: e.matmul(
                             pz[:], A4[:, kt * 128:(kt + 1) * 128],
                             B4[:, qt * 512:(qt + 1) * 512],
                             start=True, stop=True))
                    P.do("t", [A2t, B2x], [pxy],
                         lambda e, kt=kt, qt=qt: e.matmul(
                             pxy[:, 0:512], A2t[:, kt * 128:(kt + 1) * 128],
                             B2x[:, qt * 512:(qt + 1) * 512],
                             start=True, stop=True))
                    P.do("t", [A2y, B2y], [pxy],
                         lambda e, kt=kt, qt=qt: e.matmul(
                             pxy[:, 512:1024], A2y[:, kt * 128:(kt + 1) * 128],
                             B2y[:, qt * 512:(qt + 1) * 512],
                             start=True, stop=True))
                    P.do("v", [pz, thr], [vmz],
                         lambda e, pz=pz, vmz=vmz: e.tensor_scalar(
                             vmz[:], pz[:], thr[:], None, ALU.is_le))
                    P.do("v", [pxy], [vm2z],
                         lambda e, vm2z=vm2z: e.tensor_scalar(
                             vm2z[:], pxy[:, 512:1024], 0.0,
                             None, ALU.is_equal))
                    P.do("v", [pxy, vm2z], [vm2z],
                         lambda e, vm2z=vm2z: e.scalar_tensor_tensor(
                             vm2z[:], pxy[:, 0:512], 0.0, vm2z[:], ALU.is_lt,
                             ALU.mult))
                    P.do("v", [vm2z, vmz], [vm2z],
                         lambda e, vmz=vmz, vm2z=vm2z: e.tensor_sub(
                             vm2z[:], vmz[:], vm2z[:]))
                    P.do("v", [vm2z, vmz], [vmz],
                         lambda e, vmz=vmz, vm2z=vm2z: e.tensor_mul(
                             vmz[:], vmz[:], vm2z[:]))
                    dd = kt - 4 * qt
                    if dd >= 0:
                        P.do("v", [causb, vmz], [vmz],
                             lambda e, dd=dd, vmz=vmz: e.tensor_mul(
                                 vmz[:], vmz[:],
                                 causb[:, dd * 512:(dd + 1) * 512]))
                    P.do("s", [ps[gb]], [PT],
                         lambda e, kt=kt, gb=gb: e.activation(
                             PT[:, kt * 512:(kt + 1) * 512], ps[gb][:], AF.Exp,
                             scale=1.0 / 32.0))
                    P.do("v", [PT, vmz], [PT],
                         lambda e, kt=kt, vmz=vmz: e.tensor_mul(
                             PT[:, kt * 512:(kt + 1) * 512],
                             PT[:, kt * 512:(kt + 1) * 512], vmz[:]))
                for kt in range(nkt):
                    P.do("t", [P.consts["onescolb"], PT], [pl],
                         lambda e, kt=kt: e.matmul(
                             pl[:], P.consts["onescolb"][:, 0:1],
                             PT[:, kt * 512:(kt + 1) * 512],
                             start=(kt == 0), stop=(kt == nkt - 1)),
                         quiet=(kt < nkt - 1))
                P.do("v", [pl], [lrow], lambda e: e.tensor_copy(lrow[:], pl[:]))
                P.do("v", [lrow], [lrec], lambda e: e.reciprocal(lrec[:], lrow[:]))
                P.do("t", [lrec, ones], [pb],
                     lambda e: e.matmul(pb[:], ones[0:1, 0:128], lrec[:],
                                        start=True, stop=True))
                P.do("v", [pb], [lb], lambda e: e.tensor_copy(lb[:], pb[:]))
                for dc in range(8):
                    dz = dc % 2
                    vo = dz * 2048
                    P.do("sy", [v_dram], [Vdc],
                         lambda e, dc=dc, vo=vo, nkt=nkt: e.dma_start(
                             out=Vdc[:, vo:vo + nkt * 128],
                             in_=bass.AP(v_dram.t.tensor, dc * 128,
                                         [[1024, 128], [128 * 1024, nkt],
                                          [1, 128]])))
                    for kt in range(nkt):
                        P.do("t", [Vdc, PT], [pdd[dz]],
                             lambda e, dz=dz, vo=vo, kt=kt: e.matmul(
                                 pdd[dz][:],
                                 Vdc[:, vo + kt * 128:vo + (kt + 1) * 128],
                                 PT[:, kt * 512:(kt + 1) * 512],
                                 start=(kt == 0), stop=(kt == nkt - 1)),
                             quiet=(kt < nkt - 1))
                    P.do("v", [pdd[dz], lb], [ctxb[dz]],
                         lambda e, dz=dz: e.tensor_tensor(
                             ctxb[dz][:], pdd[dz][:], lb[:], ALU.mult))
                    P.do("gd", [ctxb[dz]], [P.alias(ctx_dram)],
                         lambda e, dc=dc, qt=qt, dz=dz: e.dma_start(
                             out=ctx_dram.t[dc * 128:(dc + 1) * 128,
                                            qt * 512:(qt + 1) * 512],
                             in_=ctxb[dz][:]))
            P.drain_all()
            for i in range(16):
                P.do("sy", [xn_tm_dram], [xno],
                     lambda e, i=i, s=s: e.dma_start(
                         out=xno[:],
                         in_=xn_tm_dram.t[s * 2048 + i * 128:
                                          s * 2048 + (i + 1) * 128, :]))
                co = (i % 2) * 1024
                P.do("sy", [ctx_dram], [ctxi],
                     lambda e, i=i, co=co: e.dma_start(
                         out=ctxi[:, co:co + 1024],
                         in_=bass.AP(ctx_dram.t.tensor, i * 128,
                                     [[2048, 128], [128 * 2048, 8],
                                      [1, 128]])))
                for vc in range(2):
                    pq = ps[vc]
                    for k in range(8):
                        P.do("t", [ctxi, wbuf], [pq],
                             lambda e, k=k, vc=vc, pq=pq, co=co: e.matmul(
                                 pq[:],
                                 ctxi[:, co + k * 128:co + (k + 1) * 128],
                                 wbuf[:, k * 1024 + vc * 512:
                                      k * 1024 + (vc + 1) * 512],
                                 start=(k == 0), stop=False),
                             quiet=True)
                    P.do("t", [ones, brow], [pq],
                         lambda e, vc=vc, pq=pq: e.matmul(
                             pq[:], ones[0:1, 0:128],
                             brow[0:1, 3072 + vc * 512:3072 + (vc + 1) * 512],
                             start=False, stop=False),
                         quiet=True)
                    P.do("t", [RE, wfqb], [pq],
                         lambda e, i=i, s=s, vc=vc, pq=pq: e.matmul(
                             pq[:],
                             RE[:, i * 128:(i + 1) * 128],
                             wfqb[:, vc * 512:(vc + 1) * 512],
                             start=False, stop=True))
                    P.do("v", [pq, xno], [sout],
                         lambda e, vc=vc, pq=pq: e.tensor_add(
                             sout[:, vc * 512:(vc + 1) * 512], pq[:],
                             xno[:, vc * 512:(vc + 1) * 512]))
                P.do("gd", [sout], [P.alias(s1_dram)],
                     lambda e, i=i, s=s: e.dma_start(
                         out=s1_dram.t[s * 2048 + i * 128:s * 2048 + (i + 1) * 128, :],
                         in_=sout[:]))
        P.drain_all()


def emit_gates1(P, xred_dram, wg_dram, gtab_dram):
    nc = P.nc
    identf = P.consts["identf"]
    with contextlib.ExitStack() as S:
        wg = P.sbuf("gt_wg", [8, 16], F32, S)
        wgn = P.sbuf("gt_wgn", [8, 16], F32, S)
        wgt = P.sbuf("gt_wgt", [16, 8], F32, S)
        nrm = P.sbuf("gt_nrm", [8, 1], F32, S)
        xr = P.sbuf("gt_xr", [128, 16], F32, S)
        xrt = P.sbuf("gt_xrt", [16, 128], F32, S)
        lg = P.sbuf("gt_lg", [128, 8], F32, S)
        top8 = P.sbuf("gt_top8", [128, 8], F32, S)
        idx8 = P.sbuf("gt_idx8", [128, 8], U32, S)
        nml = P.sbuf("gt_nml", [128, 1], F32, S)
        ex8 = P.sbuf("gt_ex8", [128, 8], F32, S)
        gt = P.sbuf("gt_gt", [128, 2], F32, S)
        pt = P.psum("gt_pt", [16, 128], F32, S)
        ptl = P.psum("gt_ptl", [128, 8], F32, S)
        ptw = P.psum("gt_ptw", [16, 8], F32, S)
        P.do("sy", [wg_dram], [wg],
             lambda e: e.dma_start(out=wg[:], in_=wg_dram.ap()))
        P.do("v", [wg], [wgn], lambda e: e.tensor_mul(wgn[:], wg[:], wg[:]))
        P.do("v", [wgn], [nrm], lambda e: e.reduce_sum(nrm[:], wgn[:], AX.X))
        P.do("s", [nrm], [nrm], lambda e: e.activation(nrm[:], nrm[:], AF.Sqrt))
        P.do("v", [nrm], [nrm], lambda e: e.tensor_scalar_max(nrm[:], nrm[:], 1e-4))
        P.do("v", [nrm], [nrm], lambda e: e.reciprocal(nrm[:], nrm[:]))
        P.do("v", [wg, nrm], [wgn],
             lambda e: e.tensor_scalar(wgn[:], wg[:], nrm[:], None, ALU.mult))
        P.do("t", [wgn, identf], [ptw],
             lambda e: e.transpose(ptw[:], wgn[:], identf[0:8, 0:8]))
        P.do("v", [ptw], [wgt], lambda e: e.tensor_copy(wgt[:], ptw[:]))
        for i in range(64):
            P.do("sy", [xred_dram], [xr],
                 lambda e, i=i: e.dma_start(
                     out=xr[:], in_=xred_dram.t[i * 128:(i + 1) * 128, :]))
            P.do("t", [xr, identf], [pt],
                 lambda e: e.transpose(pt[:], xr[:], identf[:]))
            P.do("v", [pt], [xrt], lambda e: e.tensor_copy(xrt[:], pt[:]))
            P.do("t", [xrt, wgt], [ptl],
                 lambda e: e.matmul(ptl[:], xrt[:], wgt[:], start=True, stop=True))
            P.do("v", [ptl], [lg], lambda e: e.tensor_copy(lg[:], ptl[:]))
            P.do("v", [lg], [top8], lambda e: e.max(top8[:], lg[:]))
            P.do("v", [top8, lg], [idx8],
                 lambda e: e.max_index(idx8[:], top8[:], lg[:]))
            P.do("v", [top8], [nml],
                 lambda e: e.tensor_scalar_mul(nml[:], top8[:, 0:1], -1.0))
            P.do("s", [lg, nml], [ex8],
                 lambda e: e.activation(ex8[:], lg[:], AF.Exp, bias=nml[:]))
            P.do("v", [ex8], [nml], lambda e: e.reduce_sum(nml[:], ex8[:], AX.X))
            P.do("v", [nml], [nml], lambda e: e.reciprocal(nml[:], nml[:]))
            P.do("v", [idx8], [gt], lambda e: e.tensor_copy(gt[:, 0:1],
                                                            idx8[:, 0:1]))
            P.do("v", [nml, gt], [gt], lambda e: e.tensor_copy(gt[:, 1:2], nml[:]))
            P.do("gd", [gt], [P.alias(gtab_dram)],
                 lambda e, i=i: e.dma_start(
                     out=gtab_dram.t[i * 128:(i + 1) * 128, :], in_=gt[:]))
        P.drain_all()


def emit_cp1(P, gtab_dram, gidx_dram, rec_dram, ntiles_dram):
    nc = P.nc
    tri = P.consts["tri_strict"]
    ones = P.consts["onesf"]
    onescol = P.consts["onescol"]
    with contextlib.ExitStack() as S:
        gg = P.sbuf("c1_gg", [128, 128], F32, S)
        km = P.sbuf("c1_km", [128, 64], F32, S)
        sc = P.sbuf("c1_sc", [128, 64], F32, S)
        zer = P.sbuf("c1_zer", [128, 64], F32, S)
        off = P.sbuf("c1_off", [128, 1], F32, S)
        cnt = P.sbuf("c1_cnt", [128, 1], F32, S)
        nrow = P.sbuf("c1_nrow", [1, 8], F32, S)
        nint = P.sbuf("c1_ni", [1, 16], I32, S)
        tilf = P.sbuf("c1_tilf", [1, 8], F32, S)
        btf = P.sbuf("c1_btf", [1, 8], F32, S)
        ebase = P.sbuf("c1_eb", [128, 1], F32, S)
        tid = P.sbuf("c1_tid", [128, 64], F32, S)
        tidi = P.sbuf("c1_tidi", [128, 64], I32, S)
        pint = P.sbuf("c1_pint", [128, 64], I32, S)
        Pp = P.sbuf("c1_Pp", [128, 64], F32, S)
        psl = P.sbuf("c1_psl", [128, 64], F32, S)
        rec = P.sbuf("c1_rec", [128, 128], F32, S)
        poff = P.psum("c1_poff", [128, 1], F32, S)
        pcnt = P.psum("c1_pcnt", [1, 1], F32, S)
        pbc = P.psum("c1_pbc", [128, 1], F32, S)
        P.do("v", [], [zer], lambda e: e.memset(zer[:], 0.0))
        P.do("g", [], [tid],
             lambda e: e.iota(tid[:], pattern=[[1, 64]], base=0,
                              channel_multiplier=64,
                              allow_small_or_imprecise_dtypes=True))
        P.do("v", [tid], [tidi], lambda e: e.tensor_copy(tidi[:], tid[:]))
        P.do("sy", [gtab_dram], [gg],
             lambda e: e.dma_start(
                 out=gg[:],
                 in_=gtab_dram.t.rearrange("(p f) c -> p (f c)", p=128)))
        P.do("v", [], [Pp], lambda e: e.memset(Pp[:], 0.0))
        for ex in range(8):
            P.do("v", [gg], [km],
                 lambda e, ex=ex: e.tensor_single_scalar(
                     km[:], gg[:, 0::2], float(ex), ALU.is_equal))
            P.do("v", [km, zer], [sc],
                 lambda e: e.tensor_tensor_scan(sc[:], km[:], zer[:], 0.0,
                                                ALU.add, ALU.add))
            P.do("t", [tri, sc], [poff],
                 lambda e: e.matmul(poff[:], tri[:], sc[:, 63:64],
                                    start=True, stop=True))
            P.do("v", [poff], [off], lambda e: e.tensor_copy(off[:], poff[:]))
            P.do("v", [sc, km], [psl],
                 lambda e: e.tensor_sub(psl[:], sc[:], km[:]))
            P.do("v", [psl, off], [psl],
                 lambda e: e.tensor_scalar(psl[:], psl[:], off[:], None, ALU.add))
            P.do("v", [km], [cnt], lambda e: e.reduce_sum(cnt[:], km[:], AX.X))
            P.do("t", [cnt, onescol], [pcnt],
                 lambda e: e.matmul(pcnt[:], cnt[:], onescol[:],
                                    start=True, stop=True))
            P.do("v", [pcnt], [nrow],
                 lambda e, ex=ex: e.tensor_copy(nrow[:, ex:ex + 1], pcnt[:]))
            P.do("v", [psl, km], [psl],
                 lambda e: e.tensor_mul(psl[:], psl[:], km[:]))
            P.do("v", [psl, Pp], [Pp], lambda e: e.tensor_add(Pp[:], Pp[:], psl[:]))
        P.do("v", [nrow], [nint],
             lambda e: e.tensor_copy(nint[0:1, 0:8], nrow[:]))
        P.do("v", [nint], [nint],
             lambda e: e.tensor_scalar(nint[0:1, 0:8], nint[0:1, 0:8], 127, None,
                                       ALU.add))
        P.do("v", [nint], [nint],
             lambda e: e.tensor_scalar(nint[0:1, 0:8], nint[0:1, 0:8], 7, None,
                                       ALU.arith_shift_right))
        P.do("v", [nint], [nint],
             lambda e: e.tensor_scalar_max(nint[0:1, 0:8], nint[0:1, 0:8], 1))
        P.do("v", [nint], [tilf], lambda e: e.tensor_copy(tilf[:], nint[0:1, 0:8]))
        P.do("v", [tilf, zer], [btf],
             lambda e: e.tensor_tensor_scan(btf[:], tilf[:], zer[0:1, 0:8], 0.0,
                                            ALU.add, ALU.add))
        P.do("v", [btf, tilf], [btf], lambda e: e.tensor_sub(btf[:], btf[:], tilf[:]))
        P.do("v", [btf], [nint], lambda e: e.tensor_copy(nint[0:1, 8:16], btf[:]))
        P.do("sy", [nint], [ntiles_dram],
             lambda e: e.dma_start(out=ntiles_dram.ap(), in_=nint[:]))
        for ex in range(8):
            P.do("t", [btf, ones], [pbc],
                 lambda e, ex=ex: e.matmul(pbc[:], ones[0:1, 0:128],
                                           btf[0:1, ex:ex + 1],
                                           start=True, stop=True))
            P.do("v", [pbc], [ebase], lambda e: e.tensor_copy(ebase[:], pbc[:]))
            P.do("v", [ebase], [ebase],
                 lambda e: e.tensor_scalar_mul(ebase[:], ebase[:], 128.0))
            P.do("v", [gg], [km],
                 lambda e, ex=ex: e.tensor_single_scalar(
                     km[:], gg[:, 0::2], float(ex), ALU.is_equal))
            P.do("v", [km, ebase], [km],
                 lambda e: e.tensor_scalar(km[:], km[:], ebase[:], None,
                                           ALU.mult))
            P.do("v", [km, Pp], [Pp], lambda e: e.tensor_add(Pp[:], Pp[:], km[:]))
        P.do("v", [Pp], [pint], lambda e: e.tensor_copy(pint[:], Pp[:]))
        for f in range(64):
            P.do("gd", [pint, tidi], [P.alias(gidx_dram)],
                 lambda e, f=f: e.indirect_dma_start(
                     out=gidx_dram.ap(),
                     out_offset=IndirectOffsetOnAxis(pint[:, f:f + 1], 0),
                     in_=tidi[:, f:f + 1], in_offset=None))
        P.dma_barrier("gd")
        P.do("v", [Pp, rec], [rec],
             lambda e: e.tensor_copy(rec[:, 0::2], Pp[:]))
        P.do("v", [gg, rec], [rec],
             lambda e: e.tensor_copy(rec[:, 1::2], gg[:, 1::2]))
        P.do("sy", [rec], [rec_dram],
             lambda e: e.dma_start(
                 out=rec_dram.t.rearrange("(p f) c -> p (f c)", f=64),
                 in_=rec[:]))
        P.drain_all()


def emit_gate1(P, x1_dram, s1_dram, yslot_dram, rec_dram, out_dram):
    nc = P.nc
    with contextlib.ExitStack() as S:
        rec = [P.sbuf(f"g1_rec{z}", [128, 2], F32, S) for z in range(2)]
        i1 = [P.sbuf(f"g1_i1{z}", [128, 1], I32, S) for z in range(2)]
        g1 = [P.sbuf(f"g1_g1{z}", [128, 1024], F32, S) for z in range(2)]
        gg = [P.sbuf(f"g1_gg{z}", [128, 1024], F32, S) for z in range(2)]
        xo = [P.sbuf(f"g1_xo{z}", [128, 1024], F32, S) for z in range(2)]
        so = [P.sbuf(f"g1_so{z}", [128, 1024], F32, S) for z in range(2)]
        for z in range(2):
            P.do("v", [], [g1[z]], lambda e, z=z: e.memset(g1[z][:], 0.0))
        for i in range(64):
            z = i % 2
            rz, i1z, g1z, ggz, xoz, soz = (rec[z], i1[z], g1[z], gg[z],
                                           xo[z], so[z])
            P.do("sy", [rec_dram], [rz],
                 lambda e, i=i, rz=rz: e.dma_start(
                     out=rz[:], in_=rec_dram.t[i * 128:(i + 1) * 128, :]))
            P.do("v", [rz], [i1z],
                 lambda e, rz=rz, i1z=i1z: e.tensor_copy(i1z[:], rz[:, 0:1]))
            P.do("gd", [i1z, yslot_dram], [g1z],
                 lambda e, i1z=i1z, g1z=g1z: e.indirect_dma_start(
                     out=g1z[:], out_offset=None, in_=yslot_dram.ap(),
                     in_offset=IndirectOffsetOnAxis(i1z[:], 0)))
            P.do("v", [g1z, rz], [ggz],
                 lambda e, g1z=g1z, rz=rz, ggz=ggz: e.tensor_scalar(
                     ggz[:], g1z[:], rz[:, 1:2], None, ALU.mult))
            P.do("s", [ggz], [ggz],
                 lambda e, ggz=ggz: e.activation(ggz[:], ggz[:], AF.Silu))
            P.do("sy", [x1_dram], [xoz],
                 lambda e, i=i, xoz=xoz: e.dma_start(
                     out=xoz[:], in_=x1_dram.t[i * 128:(i + 1) * 128, :]))
            P.do("sy", [s1_dram], [soz],
                 lambda e, i=i, soz=soz: e.dma_start(
                     out=soz[:], in_=s1_dram.t[i * 128:(i + 1) * 128, :]))
            P.do("v", [soz, ggz], [ggz],
                 lambda e, soz=soz, ggz=ggz: e.tensor_mul(ggz[:], soz[:],
                                                          ggz[:]))
            P.do("v", [xoz, ggz], [xoz],
                 lambda e, xoz=xoz, ggz=ggz: e.tensor_add(xoz[:], xoz[:],
                                                          ggz[:]))
            P.do("gd", [xoz], [P.alias(out_dram)],
                 lambda e, i=i, xoz=xoz: e.dma_start(
                     out=out_dram.t[i * 128:(i + 1) * 128, :], in_=xoz[:]))
        P.drain_all()


def build_B1():
    P = Prog()
    x1 = P.dram("x1", [TLOC, D], F32, kind="ExternalInput")
    wqkvo = P.dram("wqkvo", [D, 4 * D], F32, kind="ExternalInput")
    bqkvo = P.dram("bqkvo", [4, D], F32, kind="ExternalInput")
    wtq_aug = P.dram("wtq_aug", [D, 18], F32, kind="ExternalInput")
    wfq_aug = P.dram("wfq_aug", [3, D], F32, kind="ExternalInput")
    xn1_tm = P.dram("xn1_tm", [TLOC, D], BF16, kind="ExternalOutput")
    s1 = P.dram("s1", [TLOC, D], F32, kind="ExternalOutput")
    xred = P.dram("xred", [TLOC, 16], F32, kind="ExternalOutput")
    xn1_fm = P.dram("xn1_fm", [D, TLOC], BF16)
    xn1_fm32 = P.dram("xn1_fm32", [D, TLOC], F32)
    qpd = P.dram("qpd", [2, TLOC], F32)
    causal4_d = P.dram("c_causal4", _CONSTS["causal4"].shape, F32,
                       kind="ExternalInput")
    declare_load_consts(P, _CONSTS, ["tri_strict"])
    make_identities(P)
    make_ones(P)
    make_onescol(P)
    emit_ln(P, x1, xn1_tm, xn1_fm, "1", fm32_dram=xn1_fm32)
    emit_qpxr(P, xn1_fm32, wtq_aug, qpd, xred)
    ctxd = P.dram("ctxd", [D, N], BF16)
    qd = P.dram("qd", [D, N], BF16)
    vd = P.dram("vd", [N, D], BF16)
    emit_att1(P, xn1_tm, xn1_fm, qpd, wqkvo, bqkvo, wfq_aug, causal4_d, s1,
              ctx_dram=ctxd, q_dram=qd, v_dram=vd)
    P.drain_all()
    nc = P.nc
    P.close()
    return nc


def build_B2():
    P = Prog()
    x1 = P.dram("x1", [TLOC, D], F32, kind="ExternalInput")
    s1 = P.dram("s1", [TLOC, D], F32, kind="ExternalInput")
    xn1_tm = P.dram("xn1_tm", [TLOC, D], BF16, kind="ExternalInput")
    xred = P.dram("xred", [TLOC, 16], F32, kind="ExternalInput")
    wg = P.dram("wg", [8, 16], F32, kind="ExternalInput")
    w1e = P.dram("w1e", [8 * D, 2048], F32, kind="ExternalInput")
    b1e = P.dram("b1e", [8, 2048], F32, kind="ExternalInput")
    w2e = P.dram("w2e", [8 * 2048, D], F32, kind="ExternalInput")
    b2e = P.dram("b2e", [8, D], F32, kind="ExternalInput")
    out = P.dram("out", [TLOC, D], F32, kind="ExternalOutput")
    gtab = P.dram("gtab", [TLOC, 2], F32)
    gidx1 = P.dram("gidx1", [SLOTMAX1 * 128 + 128, 1], I32)
    rec1 = P.dram("rec1", [TLOC, 2], F32)
    ntiles1 = P.dram("ntiles1", [1, 16], I32)
    yslot1 = P.dram("yslot1", [SLOTMAX1 * 128 + 128, D], F32)
    declare_load_consts(P, _CONSTS, ["tri_strict"])
    make_identities(P)
    make_ones(P)
    make_onescol(P)
    with contextlib.ExitStack() as S:
        big = P.sbuf("g1_big", [128, SLOTMAX1 + 1], I32, S)
        P.do("v", [], [big], lambda e: e.memset(big[:], 1 << 29))
        P.do("sy", [big], [gidx1],
             lambda e: e.dma_start(
                 out=gidx1.t.rearrange("(a b) c -> a (b c)", a=128), in_=big[:]))
        zrow = P.sbuf("g1_zrow", [128, D], F32, S)
        P.do("v", [], [zrow], lambda e: e.memset(zrow[:], 0.0))
        for blk in range(SLOTMAX1 + 1):
            P.do("sy", [zrow], [yslot1],
                 lambda e, blk=blk: e.dma_start(
                     out=yslot1.t[blk * 128:(blk + 1) * 128, :], in_=zrow[:]))
        P.drain_all()
    emit_gates1(P, xred, wg, gtab)
    emit_cp1(P, gtab, gidx1, rec1, ntiles1)
    emit_xp0(P, xn1_tm, w1e, b1e, w2e, b2e, gidx1, ntiles1, yslot1,
             dff=2048, slotmax=SLOTMAX1, tag="x1")
    emit_gate1(P, x1, s1, yslot1, rec1, out)
    P.drain_all()
    nc = P.nc
    P.close()
    return nc


def _shard_inputs_B1(inputs, x1_by_core):
    wqkvo = np.ascontiguousarray(np.concatenate(
        [np.asarray(inputs["L1_attn_Wq"], np.float32).T,
         np.asarray(inputs["L1_attn_Wk"], np.float32).T,
         np.asarray(inputs["L1_attn_Wv"], np.float32).T,
         np.asarray(inputs["L1_attn_Wo"], np.float32).T], 1))
    bqkvo = np.ascontiguousarray(np.stack(
        [np.asarray(inputs["L1_attn_bq"], np.float32),
         np.asarray(inputs["L1_attn_bk"], np.float32),
         np.asarray(inputs["L1_attn_bv"], np.float32),
         np.asarray(inputs["L1_attn_bo"], np.float32)], 0))
    wtq_aug = np.ascontiguousarray(np.concatenate(
        [np.asarray(inputs["L1_attn_Wtq"], np.float32).T,
         np.asarray(inputs["L1_ffn_w_red"], np.float32)], 1))
    wfq_aug = np.ascontiguousarray(np.concatenate(
        [np.asarray(inputs["L1_attn_Wfq"], np.float32).T,
         np.asarray(inputs["L1_attn_bfq"], np.float32).reshape(1, D)], 0))
    maps = []
    for c in range(NCORES):
        maps.append({
            "x1": x1_by_core[c],
            "wqkvo": wqkvo, "bqkvo": bqkvo, "wtq_aug": wtq_aug,
            "wfq_aug": wfq_aug,
            "c_causal4": _CONSTS["causal4"],
            "c_tri_strict": _CONSTS["tri_strict"],
        })
    return maps


def _shard_inputs_B2(inputs, x1_by_core, b1_res):
    maps = []
    for c in range(NCORES):
        maps.append({
            "x1": x1_by_core[c],
            "s1": b1_res[c]["s1"],
            "xn1_tm": b1_res[c]["xn1_tm"],
            "xred": b1_res[c]["xred"],
            "wg": np.ascontiguousarray(np.asarray(inputs["L1_ffn_wg"], np.float32)),
            "w1e": np.ascontiguousarray(
                np.asarray(inputs["L1_ffn_W1"], np.float32).reshape(8 * D, 2048)),
            "b1e": np.ascontiguousarray(np.asarray(inputs["L1_ffn_b1"], np.float32)),
            "w2e": np.ascontiguousarray(
                np.asarray(inputs["L1_ffn_W2"], np.float32).reshape(8 * 2048, D)),
            "b2e": np.ascontiguousarray(np.asarray(inputs["L1_ffn_b2"], np.float32)),
            "c_tri_strict": _CONSTS["tri_strict"],
        })
    return maps


def build_FUSED():
    P = Prog()
    xt = P.dram("xt", [D, TGLB], F32, kind="ExternalInput")
    xown = P.dram("xown", [TLOC, D], F32, kind="ExternalInput")
    wqkv = P.dram("wqkv", [D, 3 * D], F32, kind="ExternalInput")
    wproj = P.dram("wproj", [D, D], F32, kind="ExternalInput")
    wr_aug = P.dram("wr_aug", [D, 9], F32, kind="ExternalInput")
    w1e = P.dram("w1e", [8 * D, DFF0], F32, kind="ExternalInput")
    b1e = P.dram("b1e", [8, DFF0], F32, kind="ExternalInput")
    w2e = P.dram("w2e", [8 * DFF0, D], F32, kind="ExternalInput")
    b2e = P.dram("b2e", [8, D], F32, kind="ExternalInput")
    fw1 = P.dram("fw1", [D, DFF0], F32, kind="ExternalInput")
    fb1 = P.dram("fb1", [32, 128], F32, kind="ExternalInput")
    fw2 = P.dram("fw2", [DFF0, D], F32, kind="ExternalInput")
    fb2 = P.dram("fb2", [1, D], F32, kind="ExternalInput")
    ownrows = P.dram("ownrows", [128, 1], I32, kind="ExternalInput")
    wqkvo = P.dram("wqkvo", [D, 4 * D], F32, kind="ExternalInput")
    bqkvo = P.dram("bqkvo", [4, D], F32, kind="ExternalInput")
    wtq_aug = P.dram("wtq_aug", [D, 18], F32, kind="ExternalInput")
    wfq_aug = P.dram("wfq_aug", [3, D], F32, kind="ExternalInput")
    wg = P.dram("wg", [8, 16], F32, kind="ExternalInput")
    w1e1 = P.dram("w1e1", [8 * D, 2048], F32, kind="ExternalInput")
    b1e1 = P.dram("b1e1", [8, 2048], F32, kind="ExternalInput")
    w2e1 = P.dram("w2e1", [8 * 2048, D], F32, kind="ExternalInput")
    b2e1 = P.dram("b2e1", [8, D], F32, kind="ExternalInput")
    causal4_d = P.dram("c_causal4", _CONSTS["causal4"].shape, F32,
                       kind="ExternalInput")
    causadd4_d = P.dram("c_causadd4", _CONSTS["causadd4"].shape, F32,
                        kind="ExternalInput")
    out = P.dram("out", [TLOC, D], F32, kind="ExternalOutput")
    xn0_tm = P.dram("xn0_tm", [TLOC, D], BF16)
    xn0_fm = P.dram("xn0_fm", [D, TLOC], BF16)
    s0 = P.dram("s0", [TLOC, D], F32)
    assign = P.dram("assign", [8, TGLB], F32)
    wetab = P.dram("wetab", [TGLB, 4], F32)
    slot = P.dram("slot", [8, TGLB], F32)
    fbd = P.dram("fbd", [1, TGLB], F32)
    gidx = P.dram("gidx", [SLOTMAX0 * 128 + 128, 1], I32)
    rec = P.dram("recd", [TLOC, 8], F32)
    ntiles = P.dram("ntiles", [1, 16], I32)
    yfb = P.dram("yfbd", [TLOC, D], F32)
    yslot = P.dram("yslot", [SLOTMAX0 * 128 + 128, D], F32)
    x1 = P.dram("x1d", [TLOC, D], F32)
    xn1_tm = P.dram("xn1_tm", [TLOC, D], BF16)
    xn1_fm = P.dram("xn1_fm", [D, TLOC], BF16)
    xn1_fm32 = P.dram("xn1_fm32", [D, TLOC], F32)
    qpd = P.dram("qpd", [2, TLOC], F32)
    xred = P.dram("xred", [TLOC, 16], F32)
    s1 = P.dram("s1d", [TLOC, D], F32)
    ctxd = P.dram("ctxd", [D, N], BF16)
    qd = P.dram("qd", [D, N], BF16)
    vd = P.dram("vd", [N, D], BF16)
    gtab = P.dram("gtab", [TLOC, 2], F32)
    gidx1 = P.dram("gidx1", [SLOTMAX1 * 128 + 128, 1], I32)
    rec1 = P.dram("rec1", [TLOC, 2], F32)
    ntiles1 = P.dram("ntiles1", [1, 16], I32)
    yslot1 = P.dram("yslot1", [SLOTMAX1 * 128 + 128, D], F32)

    declare_load_consts(P, _CONSTS, ["alibi_bias", "iota512", "tri_strict",
                                     "iota8_row"])
    make_identities(P)
    make_ones(P)
    make_onescol(P)
    with contextlib.ExitStack() as S:
        big = P.sbuf("gidx_big", [128, SLOTMAX0 + 1], I32, S)
        P.do("v", [], [big], lambda e: e.memset(big[:], 1 << 29))
        P.do("sy", [big], [gidx],
             lambda e: e.dma_start(
                 out=gidx.t.rearrange("(a b) c -> a (b c)", a=128), in_=big[:]))
        big1 = P.sbuf("gidx_big1", [128, SLOTMAX1 + 1], I32, S)
        P.do("v", [], [big1], lambda e: e.memset(big1[:], 1 << 29))
        P.do("sy", [big1], [gidx1],
             lambda e: e.dma_start(
                 out=gidx1.t.rearrange("(a b) c -> a (b c)", a=128), in_=big1[:]))
        zrow = P.sbuf("ys_zrow", [128, D], F32, S)
        P.do("v", [], [zrow], lambda e: e.memset(zrow[:], 0.0))
        for blk in range(SLOTMAX0 + 1):
            P.do("sy", [zrow], [yslot],
                 lambda e, blk=blk: e.dma_start(
                     out=yslot.t[blk * 128:(blk + 1) * 128, :], in_=zrow[:]))
        for blk in range(SLOTMAX1 + 1):
            P.do("sy", [zrow], [yslot1],
                 lambda e, blk=blk: e.dma_start(
                     out=yslot1.t[blk * 128:(blk + 1) * 128, :], in_=zrow[:]))
        P.drain_all()
    emit_ln(P, xown, xn0_tm, xn0_fm, "0")
    emit_att0(P, xn0_fm, wqkv, wproj, s0, causal4_d, causadd4_d)
    emit_rt(P, xt, wr_aug, assign, wetab)
    emit_rp(P, assign, slot, fbd)
    emit_cp(P, slot, fbd, wetab, ownrows, gidx, rec, ntiles)
    emit_yfb(P, xn0_fm, fw1, fb1, fw2, fb2, yfb)
    emit_xp0(P, xn0_tm, w1e, b1e, w2e, b2e, gidx, ntiles, yslot)
    emit_gate0(P, xown, s0, yfb, yslot, rec, x1)
    emit_ln(P, x1, xn1_tm, xn1_fm, "1", fm32_dram=xn1_fm32)
    emit_qpxr(P, xn1_fm32, wtq_aug, qpd, xred)
    emit_att1(P, xn1_tm, xn1_fm, qpd, wqkvo, bqkvo, wfq_aug, causal4_d, s1,
              ctx_dram=ctxd, q_dram=qd, v_dram=vd)
    emit_gates1(P, xred, wg, gtab)
    emit_cp1(P, gtab, gidx1, rec1, ntiles1)
    emit_xp0(P, xn1_tm, w1e1, b1e1, w2e1, b2e1, gidx1, ntiles1, yslot1,
             dff=2048, slotmax=SLOTMAX1, tag="x1")
    emit_gate1(P, x1, s1, yslot1, rec1, out)
    P.drain_all()
    nc = P.nc
    P.close()
    return nc


def _shard_inputs_FUSED(inputs):
    m1 = _shard_inputs_A1(inputs)
    x = np.asarray(inputs["L0_attn_x"], dtype=np.float32).reshape(B, N, D)
    xt_full = np.ascontiguousarray(x.reshape(TGLB, D).T)
    wr_aug = np.ascontiguousarray(np.concatenate(
        [np.asarray(inputs["L0_ffn_Wr"], np.float32),
         np.ones((D, 1), np.float32)], 1))
    wqkvo = np.ascontiguousarray(np.concatenate(
        [np.asarray(inputs["L1_attn_Wq"], np.float32).T,
         np.asarray(inputs["L1_attn_Wk"], np.float32).T,
         np.asarray(inputs["L1_attn_Wv"], np.float32).T,
         np.asarray(inputs["L1_attn_Wo"], np.float32).T], 1))
    bqkvo = np.ascontiguousarray(np.stack(
        [np.asarray(inputs["L1_attn_bq"], np.float32),
         np.asarray(inputs["L1_attn_bk"], np.float32),
         np.asarray(inputs["L1_attn_bv"], np.float32),
         np.asarray(inputs["L1_attn_bo"], np.float32)], 0))
    wtq_aug = np.ascontiguousarray(np.concatenate(
        [np.asarray(inputs["L1_attn_Wtq"], np.float32).T,
         np.asarray(inputs["L1_ffn_w_red"], np.float32)], 1))
    wfq_aug = np.ascontiguousarray(np.concatenate(
        [np.asarray(inputs["L1_attn_Wfq"], np.float32).T,
         np.asarray(inputs["L1_attn_bfq"], np.float32).reshape(1, D)], 0))
    maps = []
    for c in range(NCORES):
        m = dict(m1[c])
        m.update({
            "xt": xt_full,
            "wr_aug": wr_aug,
            "w1e": np.ascontiguousarray(
                np.asarray(inputs["L0_ffn_W1"], np.float32).reshape(8 * D, DFF0)),
            "b1e": np.ascontiguousarray(np.asarray(inputs["L0_ffn_b1"], np.float32)),
            "w2e": np.ascontiguousarray(
                np.asarray(inputs["L0_ffn_W2"], np.float32).reshape(8 * DFF0, D)),
            "b2e": np.ascontiguousarray(np.asarray(inputs["L0_ffn_b2"], np.float32)),
            "fw1": np.ascontiguousarray(np.asarray(inputs["L0_ffn_fw1"], np.float32)),
            "fb1": np.ascontiguousarray(
                np.asarray(inputs["L0_ffn_fb1"], np.float32).reshape(32, 128)),
            "fw2": np.ascontiguousarray(np.asarray(inputs["L0_ffn_fw2"], np.float32)),
            "fb2": np.ascontiguousarray(
                np.asarray(inputs["L0_ffn_fb2"], np.float32).reshape(1, D)),
            "ownrows": np.asarray(
                [[(c + 8 * (p // 32)) * 32 + (p % 32)] for p in range(128)],
                np.int32),
            "wqkvo": wqkvo, "bqkvo": bqkvo, "wtq_aug": wtq_aug,
            "wfq_aug": wfq_aug,
            "wg": np.ascontiguousarray(np.asarray(inputs["L1_ffn_wg"], np.float32)),
            "w1e1": np.ascontiguousarray(
                np.asarray(inputs["L1_ffn_W1"], np.float32).reshape(8 * D, 2048)),
            "b1e1": np.ascontiguousarray(np.asarray(inputs["L1_ffn_b1"], np.float32)),
            "w2e1": np.ascontiguousarray(
                np.asarray(inputs["L1_ffn_W2"], np.float32).reshape(8 * 2048, D)),
            "b2e1": np.ascontiguousarray(np.asarray(inputs["L1_ffn_b2"], np.float32)),
            "c_tri_strict": _CONSTS["tri_strict"],
            "c_iota8_row": _CONSTS["iota8_row"],
        })
        maps.append(m)
    return maps


def kernel(**inputs):
    inputs = {k: np.asarray(v) for k, v in inputs.items()}
    r = run_bass_kernel_spmd(_get_nc("FUSED", build_FUSED),
                             _shard_inputs_FUSED(inputs),
                             core_ids=list(range(NCORES))).results
    out = np.zeros((B, N, D), np.float32)
    for c in range(NCORES):
        out[_core_seqs(c)] = np.asarray(r[c]["out"]).reshape(4, N, D)
    return out
